# version 02 (final)
# baseline (speedup 1.0000x reference)
import numpy as np
import ml_dtypes
from contextlib import ExitStack

import concourse.bass as bass
import concourse.mybir as mybir
from concourse.bass_utils import run_bass_kernel_spmd

F32 = mybir.dt.float32
BF16 = mybir.dt.bfloat16
ALU = mybir.AluOpType
ACT = mybir.ActivationFunctionType
AXX = mybir.AxisListType.X

NS, LSEQ, D = 2, 2048, 1024
T = NS * LSEQ
DC = D // 128
TG = 512
NTG = T // TG
TGS = LSEQ // TG
NTT = LSEQ // 128

NLAYERS = 4
PHASE_LIMIT = 99
CMB_STEPS = 99
ATT_STEPS = 99
NTG_LIM = 99
DEBUG_TAPS = 0
BF = ml_dtypes.bfloat16


class Sem:
    def __init__(self, es, nc, name):
        self.h = es.enter_context(nc.semaphore(name))
        self.v = 0

    def inc(self, inst, n=1):
        inst.then_inc(self.h, n)
        self.v += n
        return self.v

    def dma(self, inst):
        return self.inc(inst, 16)


_NOTRACK = ("wait_ge", "wait_op", "nop", "drain", "sem_inc", "engine_nop",
            "br", "halt")


class EngProxy:
    def __init__(self, e):
        self.__dict__["_e"] = e
        self.__dict__["last"] = None

    def __getattr__(self, n):
        f = getattr(self._e, n)
        if callable(f) and n not in _NOTRACK and not n.startswith("reg"):
            def g(*a, **k):
                r = f(*a, **k)
                self.__dict__["last"] = r
                return r
            return g
        return f


class WrapBlock:
    def __init__(self, block):
        self._block = block

    def _wrap(self, attr):
        deco = getattr(self._block, attr)

        def mydeco(fn):
            return deco(lambda e: fn(EngProxy(e)))
        return mydeco

    @property
    def tensor(self):
        return self._wrap("tensor")

    @property
    def vector(self):
        return self._wrap("vector")

    @property
    def scalar(self):
        return self._wrap("scalar")

    @property
    def gpsimd(self):
        return self._wrap("gpsimd")

    @property
    def sync(self):
        return self._wrap("sync")


class C:
    pass


def sb(c, name, shape, dt):
    c.uid += 1
    return c.es.enter_context(c.nc.sbuf_tensor(f"{name}_{c.uid}", shape, dt))


def pst(c, name, shape, dt=F32):
    c.uid += 1
    return c.es.enter_context(c.nc.psum_tensor(f"{name}_{c.uid}", shape, dt))


def tap(x):
    return x.tensor if hasattr(x, "tensor") else x


def fm_ap(dram, t0, ncols, nchunk=DC):
    return bass.AP(tap(dram), t0, [[T, 128], [128 * T, nchunk], [1, ncols]])


def row_ap(h, row_elems, part, nfree, off=0, repl=128):
    return bass.AP(tap(h), part * row_elems + off, [[0, repl], [1, nfree]])


def barrier(c):
    c.bar_n += 1
    marks = [(c.LD, c.LD.v), (c.GD, c.GD.v), (c.PE, c.PE.v),
             (c.VE, c.VE.v), (c.SE, c.SE.v)]
    for mk in (c.block.sync, c.block.gpsimd, c.block.tensor,
               c.block.vector, c.block.scalar):
        @mk
        def _(e):
            for s, v in marks:
                if v:
                    e.wait_ge(s.h, v)



def combine_alloc(c, stats_ap=None):
    return {
        "xt": sb(c, "cmb_xt", [128, DC, TG], F32),
        "gt": sb(c, "cmb_gt", [128, DC, TG], F32),
        "st": sb(c, "cmb_st", [128, DC, TG], F32),
        "xw": sb(c, "cmb_xw", [128, DC, TG], F32),
        "xsq": sb(c, "cmb_xsq", [128, DC, TG], F32),
        "xnb": sb(c, "cmb_xnb", [128, DC, TG], BF16),
        "ab": sb(c, "cmb_ab", [1, 8, TG], F32),
        "abb": sb(c, "cmb_abb", [128, 2, TG], F32),
        "stats": stats_ap if stats_ap is not None else pst(c, "cmb_stats", [64, TG], F32),
        "ve_sq": 0, "ve_var": 0, "ve_xn": 0, "pe": 0, "gd": 0, "ld_st": 0,
    }


def emit_combine_tg(c, tl, tg, x_src, use_s, x_dst, want_ln):
    t0 = tg * TG
    xt, gt, xw, xsq = tl["xt"], tl["gt"], tl["xw"], tl["xsq"]
    ab, abb, stats, xnb = tl["ab"], tl["abb"], tl["stats"], tl["xnb"]

    @c.block.sync
    def _(e):
        if tl["ve_sq"]:
            e.wait_ge(c.VE.h, tl["ve_sq"])
        c.LD.dma(e.dma_start(out=xt[:], in_=fm_ap(x_src, t0, TG)))
        if use_s:
            c.LD.dma(e.dma_start(out=gt[:], in_=fm_ap(c.d["g_sp"], t0, TG)))
            c.LD.dma(e.dma_start(out=tl["st"][:], in_=fm_ap(c.d["s_sp"], t0, TG)))
    ld_mark = c.LD.v

    @c.block.vector
    def _(e):
        e.wait_ge(c.LD.h, ld_mark)
        if tl["pe"]:
            e.wait_ge(c.PE.h, tl["pe"])
        if use_s:
            e.tensor_tensor(out=xw[:], in0=tl["st"][:], in1=gt[:], op=ALU.mult)
            e.tensor_tensor(out=xw[:], in0=xw[:], in1=xt[:], op=ALU.add)
        else:
            e.tensor_copy(out=xw[:], in_=xt[:])
        if want_ln:
            e.tensor_tensor(out=xsq[:], in0=xw[:], in1=xw[:], op=ALU.mult)
        c.VE.inc(e.last)
    tl["ve_sq"] = c.VE.v

    if CMB_STEPS < 2:
        return
    if x_dst is not None:
        @c.block.sync
        def _(e):
            e.wait_ge(c.VE.h, tl["ve_sq"])
            c.LD.dma(e.dma_start(out=fm_ap(x_dst, t0, TG), in_=xw[:]))
        tl["ld_st"] = c.LD.v

    if not want_ln:
        return

    if CMB_STEPS < 3:
        return

    @c.block.tensor
    def _(e):
        e.wait_ge(c.VE.h, tl["ve_sq"])
        for dc in range(DC):
            e.matmul(stats[0:1, :], c.ones_f[:], xw[:, dc, :],
                     start=(dc == 0), stop=(dc == DC - 1))
        for dc in range(DC):
            e.matmul(stats[32:33, :], c.ones_f[:], xsq[:, dc, :],
                     start=(dc == 0), stop=(dc == DC - 1))
        c.PE.inc(e.last)
    tl["pe"] = c.PE.v

    @c.block.vector
    def _(e):
        e.wait_ge(c.PE.h, tl["pe"])
        e.tensor_scalar(out=ab[0:1, 0, :], in0=stats[0:1, :], scalar1=1.0 / D,
                        scalar2=None, op0=ALU.mult)
        e.tensor_scalar(out=ab[0:1, 1, :], in0=stats[32:33, :], scalar1=1.0 / D,
                        scalar2=None, op0=ALU.mult)
        e.tensor_tensor(out=ab[0:1, 2, :], in0=ab[0:1, 0, :], in1=ab[0:1, 0, :], op=ALU.mult)
        e.tensor_tensor(out=ab[0:1, 2, :], in0=ab[0:1, 1, :], in1=ab[0:1, 2, :], op=ALU.subtract)
        e.tensor_scalar(out=ab[0:1, 2, :], in0=ab[0:1, 2, :], scalar1=1e-5,
                        scalar2=None, op0=ALU.add)
        e.reciprocal(out=ab[0:1, 2, :], in_=ab[0:1, 2, :])
        c.VE.inc(e.last)
    tl["ve_var"] = c.VE.v

    if CMB_STEPS < 4:
        return

    @c.block.scalar
    def _(e):
        e.wait_ge(c.VE.h, tl["ve_var"])
        if tl["gd"]:
            e.wait_ge(c.GD.h, tl["gd"])
        e.activation(out=ab[0:1, 3, :], in_=ab[0:1, 2, :], func=ACT.Sqrt)
        c.SE.inc(e.last)
    se_mark = c.SE.v

    @c.block.vector
    def _(e):
        e.wait_ge(c.SE.h, se_mark)
        e.tensor_tensor(out=ab[0:1, 4, :], in0=ab[0:1, 0, :], in1=ab[0:1, 3, :], op=ALU.mult)
        e.tensor_scalar(out=ab[0:1, 4, :], in0=ab[0:1, 4, :], scalar1=-1.0,
                        scalar2=None, op0=ALU.mult)
        c.VE.inc(e.last)
    ve_b = c.VE.v

    if CMB_STEPS < 5:
        return

    @c.block.gpsimd
    def _(e):
        e.wait_ge(c.VE.h, ve_b)
        if tl["gd"]:
            e.wait_ge(c.GD.h, tl["gd"])
        c.GD.dma(e.dma_start(out=c.d["bc_scr"][0:2, :], in_=ab[0:1, 3:5, :]))
        e.wait_ge(c.GD.h, c.GD.v)
        c.GD.dma(e.dma_start(out=abb[:, 0, :],
                             in_=row_ap(c.d["bc_scr"], TG, 0, TG)))
        c.GD.dma(e.dma_start(out=abb[:, 1, :],
                             in_=row_ap(c.d["bc_scr"], TG, 1, TG)))
    tl["gd"] = c.GD.v

    if CMB_STEPS < 6:
        return

    @c.block.vector
    def _(e):
        e.wait_ge(c.GD.h, tl["gd"])
        a_b = bass.AP(abb, 0, [[2 * TG, 128], [0, DC], [1, TG]])
        b_b = bass.AP(abb, TG, [[2 * TG, 128], [0, DC], [1, TG]])
        e.tensor_tensor(out=xsq[:], in0=xw[:], in1=a_b, op=ALU.mult)
        e.tensor_tensor(out=xnb[:], in0=xsq[:], in1=b_b, op=ALU.add)
        c.VE.inc(e.last)
    tl["ve_xn"] = c.VE.v

    @c.block.sync
    def _(e):
        e.wait_ge(c.VE.h, tl["ve_xn"])
        c.LD.dma(e.dma_start(out=fm_ap(c.d["xn_sp"], t0, TG), in_=xnb[:]))


def phase_combine(c, x_src, use_s, x_dst, want_ln):
    with ExitStack() as es:
        c.es = es
        tl = combine_alloc(c)
        for tg in range(min(NTG, NTG_LIM)):
            emit_combine_tg(c, tl, tg, x_src, use_s, x_dst, want_ln)
        barrier(c)



def phase_l0_router(c):
    with ExitStack() as es:
        c.es = es
        wfm = sb(c, "rt_wfm", [8, T], F32)
        wfm_bf = sb(c, "rt_wfmb", [8, T], BF16)
        xnt = sb(c, "rt_xnt", [128, DC, TG], BF16)
        wr_sb = sb(c, "rt_wr", [128, DC, 8], BF16)
        br_sb = sb(c, "rt_br", [8, 1], F32)
        logit = sb(c, "rt_logit", [8, T], F32)
        ltm = sb(c, "rt_ltm", [128, 32, 8], F32)
        sc2 = sb(c, "rt_sc", [128, 32, 6], F32)
        wtm = sb(c, "rt_wtm", [128, 32, 8], F32)
        pl = pst(c, "rt_pl", [8, TG], F32)
        ptT = pst(c, "rt_ptT", [128, 32, 8], F32)
        ptW = pst(c, "rt_ptW", [8, 8, 128], F32)

        @c.block.gpsimd
        def _(e):
            src = bass.AP(tap(c.d["wr"]), 0, [[8, 128], [128 * 8, DC], [1, 8]])
            c.GD.dma(e.dma_start(out=wr_sb[:], in_=src))
        gd0 = c.GD.v

        @c.block.sync
        def _(e):
            c.LD.dma(e.dma_start(out=br_sb[:], in_=c.d["br_col"]))
        ld0 = c.LD.v

        se_prev = 0
        pe_mm = 0
        for tg in range(NTG):
            t0 = tg * TG

            @c.block.sync
            def _(e):
                if tg > 0:
                    e.wait_ge(c.PE.h, pe_mm)
                c.LD.dma(e.dma_start(out=xnt[:], in_=fm_ap(c.d["xn_sp"], t0, TG)))
            ld_mark = c.LD.v

            @c.block.tensor
            def _(e):
                e.wait_ge(c.LD.h, max(ld_mark, ld0))
                e.wait_ge(c.GD.h, gd0)
                if se_prev:
                    e.wait_ge(c.SE.h, se_prev)
                for dc in range(DC):
                    e.matmul(pl[:, :], wr_sb[:, dc, :], xnt[:, dc, :],
                             start=(dc == 0), stop=(dc == DC - 1))
                c.PE.inc(e.last)
            pe_mm = c.PE.v

            @c.block.scalar
            def _(e):
                e.wait_ge(c.PE.h, pe_mm)
                e.activation(out=logit[:, t0:t0 + TG], in_=pl[:, :],
                             func=ACT.Identity, bias=br_sb[:, 0:1])
                c.SE.inc(e.last)
            se_prev = c.SE.v

        @c.block.tensor
        def _(e):
            e.wait_ge(c.SE.h, se_prev)
            for blk in range(32):
                e.transpose(ptT[:, blk, :], logit[:, blk * 128:(blk + 1) * 128],
                            c.id_f[0:8, 0:8])
            c.PE.inc(e.last)
        pe_t = c.PE.v

        @c.block.vector
        def _(e):
            e.wait_ge(c.PE.h, pe_t)
            e.tensor_copy(out=ltm[:], in_=ptT[:])
            e.reduce_max(out=sc2[:, :, 0:1], in_=ltm[:], axis=AXX)
            m1b = bass.AP(sc2, 0, [[32 * 6, 128], [6, 32], [0, 8]])
            e.tensor_tensor(out=wtm[:], in0=ltm[:], in1=m1b, op=ALU.is_equal)
            e.scalar_tensor_tensor(out=wtm[:], in0=wtm[:], scalar=-1e30,
                                   in1=ltm[:], op0=ALU.mult, op1=ALU.add)
            e.reduce_max(out=sc2[:, :, 2:3], in_=wtm[:], axis=AXX)
            m2b = bass.AP(sc2, 2, [[32 * 6, 128], [6, 32], [0, 8]])
            e.tensor_tensor(out=wtm[:], in0=ltm[:], in1=m2b, op=ALU.is_ge)
            e.tensor_tensor(out=ltm[:], in0=ltm[:], in1=m1b, op=ALU.subtract)
            c.VE.inc(e.last)
        ve_g1 = c.VE.v

        @c.block.scalar
        def _(e):
            e.wait_ge(c.VE.h, ve_g1)
            e.activation(out=ltm[:], in_=ltm[:], func=ACT.Exp)
            c.SE.inc(e.last)
        se_exp = c.SE.v

        @c.block.vector
        def _(e):
            e.wait_ge(c.SE.h, se_exp)
            e.tensor_tensor(out=wtm[:], in0=wtm[:], in1=ltm[:], op=ALU.mult)
            e.reduce_sum(out=sc2[:, :, 3:4], in_=wtm[:], axis=AXX)
            e.reciprocal(out=sc2[:, :, 4:5], in_=sc2[:, :, 3:4])
            rsb = bass.AP(sc2, 4, [[32 * 6, 128], [6, 32], [0, 8]])
            e.tensor_tensor(out=wtm[:], in0=wtm[:], in1=rsb, op=ALU.mult)
            c.VE.inc(e.last)
        ve_w = c.VE.v

        se_wv = 0
        for rnd in range(4):
            @c.block.tensor
            def _(e):
                e.wait_ge(c.VE.h, ve_w)
                if se_wv:
                    e.wait_ge(c.SE.h, se_wv)
                for b in range(8):
                    blk = rnd * 8 + b
                    e.transpose(ptW[:, b, :], wtm[:, blk, :], c.id_f[:])
                c.PE.inc(e.last)
            pe_w = c.PE.v

            @c.block.scalar
            def _(e):
                e.wait_ge(c.PE.h, pe_w)
                dst = bass.AP(wfm, rnd * 8 * 128, [[T, 8], [128, 8], [1, 128]])
                e.copy(out=dst, in_=ptW[:, :, :])
                c.SE.inc(e.last)
            se_wv = c.SE.v

        @c.block.vector
        def _(e):
            e.wait_ge(c.SE.h, se_wv)
            e.tensor_copy(out=wfm_bf[:], in_=wfm[:])
            c.VE.inc(e.last)
        ve_wbf = c.VE.v

        @c.block.sync
        def _(e):
            e.wait_ge(c.VE.h, ve_wbf)
            c.LD.dma(e.dma_start(out=c.d["wfm_dram"][:], in_=wfm_bf[:]))
        barrier(c)


def phase_l0_moe(c):
    with ExitStack() as es:
        c.es = es
        w1r = sb(c, "moe_w1", [128, DC, 4096], BF16)
        w2r = sb(c, "moe_w2", [128, 32, 1024], BF16)
        xnt = [sb(c, "moe_xnt", [128, DC, TG], BF16) for _ in range(2)]
        hsb = sb(c, "moe_h", [128, 32, TG], BF16)
        wbc = sb(c, "moe_wbc", [128, TG], BF16)
        acc = sb(c, "moe_acc", [128, DC, TG], F32)
        b1f = sb(c, "moe_b1", [128, 256], F32)
        b2s = sb(c, "moe_b2", [8, 1024], BF16)
        wfb = sb(c, "moe_wfb", [8, TG], BF16)
        psH = [pst(c, "moe_psH", [128, TG], F32) for _ in range(2)]
        psO = [pst(c, "moe_psO", [128, TG], F32) for _ in range(2)]

        @c.block.gpsimd
        def _(e):
            c.GD.dma(e.dma_start(out=b2s[:], in_=c.d["b2"]))
        gd_b2 = c.GD.v

        @c.block.sync
        def _(e):
            c.LD.dma(e.dma_start(out=b1f[:], in_=c.d["b1_fm"]))
        ld_b1 = c.LD.v

        hc_se = {}
        hc_ve = {}
        acc_gd = {}
        last_acc_gd = 0
        pe_xn = {}
        pe_w_used = 0
        pe_w2_prev_tg = 0
        ve_wbc = {}

        for e_i in range(8):
            @c.block.gpsimd
            def _(g):
                if pe_w_used:
                    g.wait_ge(c.PE.h, pe_w_used)
                src1 = bass.AP(tap(c.d["w1"]), e_i * D * 4096,
                               [[4096, 128], [128 * 4096, DC], [1, 4096]])
                c.GD.dma(g.dma_start(out=w1r[:], in_=src1))
                src2 = bass.AP(tap(c.d["w2"]), e_i * 4096 * D,
                               [[D, 128], [128 * D, 32], [1, D]])
                c.GD.dma(g.dma_start(out=w2r[:], in_=src2))
            gd_w = c.GD.v

            for tg in range(NTG):
                t0 = tg * TG

                @c.block.gpsimd
                def _(g):
                    pk = pe_xn.get((e_i, tg - 2)) or pe_xn.get((e_i - 1, NTG + tg - 2))
                    if pk:
                        g.wait_ge(c.PE.h, pk)
                    c.GD.dma(g.dma_start(out=xnt[tg % 2][:],
                                         in_=fm_ap(c.d["xn_sp"], t0, TG)))
                    vk = ve_wbc.get((e_i, tg - 1)) or ve_wbc.get((e_i - 1, NTG + tg - 1))
                    if vk:
                        g.wait_ge(c.VE.h, vk)
                    c.GD.dma(g.dma_start(out=wbc[:],
                                         in_=row_ap(c.d["wfm_dram"], T, e_i, TG, off=t0)))
                    if e_i == 0:
                        src_w = bass.AP(tap(c.d["wfm_dram"]), t0, [[T, 8], [1, TG]])
                        c.GD.dma(g.dma_start(out=wfb[:], in_=src_w))
                gd_xt = c.GD.v

                for hc in range(32):
                    @c.block.tensor
                    def _(t):
                        if hc == 0:
                            t.wait_ge(c.GD.h, max(gd_w, gd_xt))
                        if hc_se.get(hc - 2):
                            t.wait_ge(c.SE.h, hc_se[hc - 2])
                        for dc in range(DC):
                            t.matmul(psH[hc % 2][:, :],
                                     w1r[:, dc, hc * 128:(hc + 1) * 128],
                                     xnt[tg % 2][:, dc, :],
                                     start=(dc == 0), stop=(dc == DC - 1))
                        c.PE.inc(t.last)
                    pe_h = c.PE.v
                    if hc == 31:
                        pe_xn[(e_i, tg)] = pe_h

                    @c.block.scalar
                    def _(s):
                        s.wait_ge(c.PE.h, pe_h)
                        if e_i == 0 and tg == 0 and hc == 0:
                            s.wait_ge(c.LD.h, ld_b1)
                        if pe_w2_prev_tg:
                            s.wait_ge(c.PE.h, pe_w2_prev_tg)
                        s.activation(out=hsb[:, hc, :], in_=psH[hc % 2][:, :],
                                     func=ACT.Gelu,
                                     bias=b1f[:, e_i * 32 + hc:e_i * 32 + hc + 1])
                        c.SE.inc(s.last)
                    hc_se[hc] = c.SE.v

                    @c.block.vector
                    def _(v):
                        v.wait_ge(c.SE.h, hc_se[hc])
                        if hc == 0:
                            v.wait_ge(c.GD.h, gd_xt)
                        v.tensor_tensor(out=hsb[:, hc, :], in0=hsb[:, hc, :],
                                        in1=wbc[:], op=ALU.mult)
                        c.VE.inc(v.last)
                    hc_ve[hc] = c.VE.v
                ve_wbc[(e_i, tg)] = c.VE.v

                ve_evac = {}
                for dc in range(DC):
                    @c.block.tensor
                    def _(t):
                        if dc == 0:
                            t.wait_ge(c.VE.h, hc_ve[31])
                            if e_i == 0:
                                t.wait_ge(c.GD.h, gd_b2)
                        if ve_evac.get(dc - 2):
                            t.wait_ge(c.VE.h, ve_evac[dc - 2])
                        first = True
                        if e_i == 0:
                            t.matmul(psO[dc % 2][:, :],
                                     b2s[:, dc * 128:(dc + 1) * 128],
                                     wfb[:, :],
                                     start=True, stop=False)
                            first = False
                        for hc in range(32):
                            t.matmul(psO[dc % 2][:, :],
                                     w2r[:, hc, dc * 128:(dc + 1) * 128],
                                     hsb[:, hc, :],
                                     start=first, stop=(hc == 31))
                            first = False
                        c.PE.inc(t.last)
                    pe_o = c.PE.v

                    @c.block.vector
                    def _(v):
                        v.wait_ge(c.PE.h, pe_o)
                        if dc == 0 and last_acc_gd:
                            v.wait_ge(c.GD.h, last_acc_gd)
                        v.tensor_copy(out=acc[:, dc, :], in_=psO[dc % 2][:, :])
                        c.VE.inc(v.last)
                    ve_evac[dc] = c.VE.v
                pe_w2_prev_tg = c.PE.v
                if tg == NTG - 1:
                    pe_w_used = c.PE.v

                @c.block.gpsimd
                def _(g):
                    g.wait_ge(c.VE.h, ve_evac[DC - 1])
                    if acc_gd.get(tg):
                        g.wait_ge(c.GD.h, acc_gd[tg])
                    op = ALU.bypass if e_i == 0 else ALU.add
                    c.GD.dma(g.dma_start(out=fm_ap(c.d["gacc"], t0, TG),
                                         in_=acc[:], accum_op=op))
                acc_gd[tg] = c.GD.v
                last_acc_gd = c.GD.v

        barrier(c)


def phase_silu_g(c, src):
    with ExitStack() as es:
        c.es = es
        tin = [sb(c, "sg_in", [128, DC, TG], F32) for _ in range(2)]
        tout = [sb(c, "sg_out", [128, DC, TG], F32) for _ in range(2)]
        se_g, st_g = {}, {}
        for tg in range(NTG):
            t0 = tg * TG

            @c.block.sync
            def _(e):
                if se_g.get(tg - 2):
                    e.wait_ge(c.SE.h, se_g[tg - 2])
                c.LD.dma(e.dma_start(out=tin[tg % 2][:], in_=fm_ap(src, t0, TG)))
            ld_g = c.LD.v

            @c.block.scalar
            def _(e):
                e.wait_ge(c.LD.h, ld_g)
                if st_g.get(tg - 2):
                    e.wait_ge(c.LD.h, st_g[tg - 2])
                e.activation(out=tout[tg % 2][:], in_=tin[tg % 2][:], func=ACT.Silu)
                c.SE.inc(e.last)
            se_g[tg] = c.SE.v

            @c.block.sync
            def _(e):
                e.wait_ge(c.SE.h, se_g[tg])
                c.LD.dma(e.dma_start(out=fm_ap(c.d["g_sp"], t0, TG),
                                     in_=tout[tg % 2][:]))
            st_g[tg] = c.LD.v
        barrier(c)



def phase_proj(c, src, w_dram, dst, din, dnout, bias=None, act="copy",
               scale=None, t_lim=None, out_dt=F32):
    ic = din // 128
    oc = dnout // 128
    tot = T if t_lim is None else t_lim
    ntg = tot // TG
    with ExitStack() as es:
        c.es = es
        wsb = sb(c, "pj_w", [128, ic, dnout], BF16)
        xt = [sb(c, "pj_x", [128, ic, TG], BF16) for _ in range(2)]
        ot = [sb(c, "pj_o", [128, TG], out_dt) for _ in range(2)]
        et = sb(c, "pj_e", [128, 3, TG], F32) if act == "elu1" else None
        bcol = sb(c, "pj_b", [128, oc], F32) if bias is not None else None
        psA = [pst(c, "pj_ps", [128, TG], F32) for _ in range(2)]

        @c.block.gpsimd
        def _(e):
            wap = bass.AP(tap(w_dram), 0, [[dnout, 128], [128 * dnout, ic], [1, dnout]])
            c.GD.dma(e.dma_start(out=wsb[:], in_=wap))
            if bias is not None:
                c.GD.dma(e.dma_start(out=bcol[:], in_=bias))
        gd_w = c.GD.v

        if bias is not None and scale is not None:
            @c.block.scalar
            def _(s):
                s.wait_ge(c.GD.h, gd_w)
                s.activation(out=bcol[:], in_=bcol[:], func=ACT.Identity, scale=scale)
                c.SE.inc(s.last)
            se_b = c.SE.v

            @c.block.tensor
            def _(t):
                t.wait_ge(c.SE.h, se_b)

        pe_mm = {}
        se_ev = {}
        ve_ev = {}
        st_o = {}
        for tg in range(ntg):
            t0 = tg * TG

            @c.block.sync
            def _(e):
                pk = pe_mm.get((tg - 2, oc - 1))
                if pk:
                    e.wait_ge(c.PE.h, pk)
                c.LD.dma(e.dma_start(out=xt[tg % 2][:],
                                     in_=fm_ap(src, t0, TG, nchunk=ic)))
            ld_x = c.LD.v

            for o_i in range(oc):
                @c.block.tensor
                def _(t):
                    if tg == 0 and o_i == 0:
                        t.wait_ge(c.GD.h, gd_w)
                    if o_i == 0:
                        t.wait_ge(c.LD.h, ld_x)
                    pk = se_ev.get((tg, o_i - 2)) or (se_ev.get((tg - 1, oc + o_i - 2))
                                                      if tg > 0 else None)
                    if pk:
                        t.wait_ge(c.SE.h, pk)
                    if act == "elu1":
                        vk = ve_ev.get((tg, o_i - 2)) or (ve_ev.get((tg - 1, oc + o_i - 2))
                                                          if tg > 0 else None)
                        if vk:
                            t.wait_ge(c.VE.h, vk)
                    for i_i in range(ic):
                        t.matmul(psA[o_i % 2][:, :],
                                 wsb[:, i_i, o_i * 128:(o_i + 1) * 128],
                                 xt[tg % 2][:, i_i, :],
                                 start=(i_i == 0), stop=(i_i == ic - 1))
                    c.PE.inc(t.last)
                pe_mm[(tg, o_i)] = c.PE.v

                if act == "elu1":
                    @c.block.vector
                    def _(v):
                        v.wait_ge(c.PE.h, pe_mm[(tg, o_i)])
                        sk = se_ev.get((tg, o_i - 1)) or (se_ev.get((tg - 1, oc - 1))
                                                          if tg > 0 else None)
                        if sk:
                            v.wait_ge(c.SE.h, sk)
                        if bias is not None:
                            v.tensor_scalar(out=et[:, 2, :], in0=psA[o_i % 2][:, :],
                                            scalar1=bcol[:, o_i:o_i + 1], scalar2=None,
                                            op0=ALU.add)
                        else:
                            v.tensor_copy(out=et[:, 2, :], in_=psA[o_i % 2][:, :])
                        v.tensor_scalar(out=et[:, 0, :], in0=et[:, 2, :],
                                        scalar1=0.0, scalar2=None, op0=ALU.min)
                        v.tensor_scalar(out=et[:, 1, :], in0=et[:, 2, :],
                                        scalar1=0.0, scalar2=None, op0=ALU.max)
                        c.VE.inc(v.last)
                    ve_mm = c.VE.v

                    @c.block.scalar
                    def _(s):
                        s.wait_ge(c.VE.h, ve_mm)
                        s.activation(out=et[:, 0, :], in_=et[:, 0, :], func=ACT.Exp)
                        c.SE.inc(s.last)
                    se_x = c.SE.v

                    @c.block.vector
                    def _(v):
                        v.wait_ge(c.SE.h, se_x)
                        lk = st_o.get((tg, o_i - 2)) or (st_o.get((tg - 1, oc + o_i - 2))
                                                         if tg > 0 else None)
                        if lk:
                            v.wait_ge(c.LD.h, lk)
                        v.tensor_tensor(out=ot[o_i % 2][:], in0=et[:, 0, :],
                                        in1=et[:, 1, :], op=ALU.add)
                        c.VE.inc(v.last)
                    ve_ev[(tg, o_i)] = c.VE.v
                    se_ev[(tg, o_i)] = se_x
                    ev_sem, ev_mark = c.VE, ve_ev[(tg, o_i)]
                else:
                    @c.block.scalar
                    def _(s):
                        s.wait_ge(c.PE.h, pe_mm[(tg, o_i)])
                        lk = st_o.get((tg, o_i - 2)) or (st_o.get((tg - 1, oc + o_i - 2))
                                                         if tg > 0 else None)
                        if lk:
                            s.wait_ge(c.LD.h, lk)
                        fn = {"copy": ACT.Identity, "gelu": ACT.Gelu,
                              "silu": ACT.Silu}[act]
                        kw = {}
                        if bias is not None:
                            kw["bias"] = bcol[:, o_i:o_i + 1]
                        if scale is not None:
                            kw["scale"] = scale
                        s.activation(out=ot[o_i % 2][:], in_=psA[o_i % 2][:, :],
                                     func=fn, **kw)
                        c.SE.inc(s.last)
                    se_ev[(tg, o_i)] = c.SE.v
                    ev_sem, ev_mark = c.SE, se_ev[(tg, o_i)]

                @c.block.sync
                def _(e):
                    e.wait_ge(ev_sem.h, ev_mark)
                    dap = bass.AP(tap(dst), o_i * 128 * T + t0, [[T, 128], [1, TG]])
                    c.LD.dma(e.dma_start(out=dap, in_=ot[o_i % 2][:]))
                st_o[(tg, o_i)] = c.LD.v
        barrier(c)



def phase_attn_core(c, lay):
    scale = lay["scale"]
    has_b = lay.get("bv") is not None
    for seq in range(NS):
        s0 = seq * LSEQ
        with ExitStack() as esA:
            c.es = esA
            Qf = sb(c, "at_Q", [128, DC, LSEQ], BF16)
            Kf = sb(c, "at_K", [128, DC, LSEQ], BF16)
            Vt = sb(c, "at_V", [128, NTT, D], BF16)
            bvb = sb(c, "at_bvb", [128, D], F32)

            @c.block.sync
            def _(e):
                c.LD.dma(e.dma_start(out=Qf[:], in_=fm_ap(c.d[lay["q_dram"]], s0, LSEQ)))
                c.LD.dma(e.dma_start(out=Kf[:], in_=fm_ap(c.d[lay["k_dram"]], s0, LSEQ)))
            ld_qk = c.LD.v

            with ExitStack() as esB:
                c.es = esB
                wv_sl = sb(c, "at_wv", [128, DC, D], BF16)
                xnt = [sb(c, "at_xnt", [128, DC, 128], BF16) for _ in range(2)]
                psV = [pst(c, "at_psV", [128, D], F32) for _ in range(2)]
                c.es = esA

                @c.block.gpsimd
                def _(g):
                    src = bass.AP(tap(c.d[lay["wv"]]), 0,
                                  [[D, 128], [128 * D, DC], [1, D]])
                    c.GD.dma(g.dma_start(out=wv_sl[:], in_=src))
                    if has_b:
                        c.GD.dma(g.dma_start(out=bvb[:],
                                             in_=bass.AP(tap(c.d[lay["bv"]]), 0,
                                                         [[0, 128], [1, D]])))
                gd_wv = c.GD.v

                pe_v = {}
                ve_v = {}
                for tt in range(NTT):
                    @c.block.sync
                    def _(e):
                        if pe_v.get(tt - 2):
                            e.wait_ge(c.PE.h, pe_v[tt - 2])
                        c.LD.dma(e.dma_start(out=xnt[tt % 2][:],
                                             in_=fm_ap(c.d["xn_sp"], s0 + tt * 128, 128)))
                    ld_x = c.LD.v

                    @c.block.tensor
                    def _(t):
                        t.wait_ge(c.LD.h, ld_x)
                        if tt == 0:
                            t.wait_ge(c.GD.h, gd_wv)
                        if ve_v.get(tt - 2):
                            t.wait_ge(c.VE.h, ve_v[tt - 2])
                        for vh in range(2):
                            for dc in range(DC):
                                t.matmul(psV[tt % 2][:, vh * 512:(vh + 1) * 512],
                                         xnt[tt % 2][:, dc, :],
                                         wv_sl[:, dc, vh * 512:(vh + 1) * 512],
                                         start=(dc == 0), stop=(dc == DC - 1))
                        c.PE.inc(t.last)
                    pe_v[tt] = c.PE.v

                    @c.block.vector
                    def _(v):
                        v.wait_ge(c.PE.h, pe_v[tt])
                        if has_b:
                            v.tensor_tensor(out=Vt[:, tt, :], in0=psV[tt % 2][:, :],
                                            in1=bvb[:], op=ALU.add)
                        else:
                            v.tensor_copy(out=Vt[:, tt, :], in_=psV[tt % 2][:, :])
                        c.VE.inc(v.last)
                    ve_v[tt] = c.VE.v
                barrier(c)
                c.es = esA

            P = [sb(c, "at_P", [128, LSEQ], BF16) for _ in range(2)]
            PT = sb(c, "at_PT", [128, NTT, 128], BF16)
            fl = sb(c, "at_fl", [128, 4], F32)
            ob = sb(c, "at_ob", [64, 2, 128], BF16)
            psS = pst(c, "at_psS", [128, LSEQ], F32)
            psT = pst(c, "at_psT", [128, 1024], BF16)
            psO = pst(c, "at_psO", [64, 128], F32)

            se_e = {}
            ve_n = {}
            se_pt = {}
            pe_pv = {}
            se_o = {}
            st_o = {}
            prev_e = 0
            prev_o = 0
            prev_pv = 0
            for h in range(16):
                qrow = (h % 2) * 64
                qdc = h // 2
                for ii in range(NTT + 1):
                    if ii < NTT:
                        i = ii
                        ext = (i + 1) * 128

                        @c.block.tensor
                        def _(t):
                            if prev_e:
                                t.wait_ge(c.SE.h, prev_e)
                            else:
                                t.wait_ge(c.LD.h, ld_qk)
                            for j in range((ext + TG - 1) // TG):
                                n = min(TG, ext - j * TG)
                                t.matmul(psS[:, j * TG:j * TG + n],
                                         Qf[qrow:qrow + 64, qdc, i * 128:(i + 1) * 128],
                                         Kf[qrow:qrow + 64, qdc, j * TG:j * TG + n],
                                         start=True, stop=True)
                            c.PE.inc(t.last)
                        pe_s = c.PE.v

                        @c.block.vector
                        def _(v):
                            v.wait_ge(c.PE.h, pe_s)
                            v.tensor_tensor(out=psS[:, i * 128:(i + 1) * 128],
                                            in0=psS[:, i * 128:(i + 1) * 128],
                                            in1=c.cm[:], op=ALU.add)
                            v.reduce_max(out=fl[:, 1:2], in_=psS[:, 0:ext], axis=AXX,
                                         negate=True)
                            c.VE.inc(v.last)
                        ve_m = c.VE.v

                        @c.block.scalar
                        def _(s):
                            s.wait_ge(c.VE.h, ve_m)
                            s.activation(out=P[i % 2][:, 0:ext], in_=psS[:, 0:ext],
                                         func=ACT.Exp, bias=fl[:, 1:2],
                                         accum_out=fl[:, 2:3])
                            c.SE.inc(s.last)
                        se_e[(h, i)] = c.SE.v
                        prev_e = c.SE.v

                        @c.block.vector
                        def _(v):
                            v.wait_ge(c.SE.h, se_e[(h, i)])
                            v.reciprocal(out=fl[:, 3:4], in_=fl[:, 2:3])
                            c.VE.inc(v.last)
                        ve_r = c.VE.v

                        @c.block.scalar
                        def _(s):
                            s.wait_ge(c.VE.h, ve_r)
                            s.activation(out=P[i % 2][:, 0:ext], in_=P[i % 2][:, 0:ext],
                                         func=ACT.Identity, scale=fl[:, 3:4])
                            c.SE.inc(s.last)
                        ve_n[(h, i)] = ("SE", c.SE.v)

                    if ii >= 1:
                        j = ii - 1
                        nb = j + 1
                        for c0 in range(0, nb, 8):
                            nblk = min(8, nb - c0)

                            @c.block.tensor
                            def _(t):
                                if c0 == 0:
                                    t.wait_ge(c.SE.h, ve_n[(h, j)][1])
                                else:
                                    t.wait_ge(c.VE.h, se_pt[(h, j, c0 - 8)])
                                for b in range(nblk):
                                    cc = c0 + b
                                    t.transpose(psT[:, b * 128:(b + 1) * 128],
                                                P[j % 2][:, cc * 128:(cc + 1) * 128],
                                                c.id_bf[:])
                                c.PE.inc(t.last)
                            pe_t = c.PE.v

                            @c.block.vector
                            def _(v):
                                v.wait_ge(c.PE.h, pe_t)
                                if prev_pv:
                                    v.wait_ge(c.PE.h, prev_pv)
                                dstp = bass.AP(PT, c0 * 128,
                                               [[NTT * 128, 128], [128, nblk], [1, 128]])
                                v.tensor_copy(out=dstp, in_=psT[:, 0:nblk * 128])
                                c.VE.inc(v.last)
                            se_pt[(h, j, c0)] = c.VE.v
                        last_pt = c.VE.v

                        @c.block.tensor
                        def _(t):
                            t.wait_ge(c.VE.h, last_pt)
                            if prev_o:
                                t.wait_ge(c.SE.h, prev_o)
                            for cc in range(j + 1):
                                t.matmul(psO[:, :],
                                         Vt[:, cc, h * 64:(h + 1) * 64],
                                         PT[:, cc, :],
                                         start=(cc == 0), stop=(cc == j))
                            c.PE.inc(t.last)
                        pe_pv[(h, j)] = c.PE.v
                        prev_pv = c.PE.v

                        @c.block.scalar
                        def _(s):
                            s.wait_ge(c.PE.h, pe_pv[(h, j)])
                            lk = st_o.get((h, j - 2)) or (st_o.get((h - 1, NTT - 2))
                                                          if h > 0 else None)
                            if lk:
                                s.wait_ge(c.LD.h, lk)
                            s.copy(out=ob[:, j % 2, :], in_=psO[:, :])
                            c.SE.inc(s.last)
                        se_o[(h, j)] = c.SE.v
                        prev_o = c.SE.v

                        @c.block.sync
                        def _(e):
                            e.wait_ge(c.SE.h, se_o[(h, j)])
                            dap = bass.AP(tap(c.d["o_att"]),
                                          (h * 64) * T + s0 + j * 128,
                                          [[T, 64], [1, 128]])
                            c.LD.dma(e.dma_start(out=dap, in_=ob[:, j % 2, :]))
                        st_o[(h, j)] = c.LD.v
            barrier(c)
        c.es = None


def phase_mha(c, lay):
    phase_proj(c, c.d["xn_sp"], c.d[lay["wq"]], c.d["q_sp"], D, D,
               bias=c.d[lay["bq"]] if lay.get("bq") else None, out_dt=BF16,
               scale=lay["scale"])
    phase_proj(c, c.d["xn_sp"], c.d[lay["wk"]], c.d["k_sp"], D, D,
               bias=c.d[lay["bk"]] if lay.get("bk") else None, out_dt=BF16)
    lay2 = dict(lay)
    lay2["q_dram"] = "q_sp"
    lay2["k_dram"] = "k_sp"
    phase_attn_core(c, lay2)
    phase_proj(c, c.d["o_att"], c.d[lay["wo"]], c.d["s_sp"], D, D,
               bias=c.d[lay["bo"]] if lay.get("bo") else None)



def emit_absmean_quant(c, w_dram, wq_dram, R, CF):
    nR = R // 128
    with ExitStack() as es:
        c.es = es
        wt = sb(c, "aq_w", [128, CF], F32)
        aw = sb(c, "aq_a", [128, CF], F32)
        sg = sb(c, "aq_s", [128, CF], F32)
        qb = sb(c, "aq_q", [128, CF], BF16)
        rs = sb(c, "aq_rs", [128, nR], F32)
        g1 = sb(c, "aq_g", [1, 2], F32)
        thr = sb(c, "aq_t", [128, 1], F32)
        psG = pst(c, "aq_ps", [1, 1], F32)

        se_a = 0
        for rc in range(nR):
            @c.block.sync
            def _(e):
                if se_a:
                    e.wait_ge(c.SE.h, se_a)
                c.LD.dma(e.dma_start(out=wt[:],
                                     in_=bass.AP(tap(w_dram), rc * 128 * CF,
                                                 [[CF, 128], [1, CF]])))
            ld_w = c.LD.v

            @c.block.scalar
            def _(s):
                s.wait_ge(c.LD.h, ld_w)
                s.activation(out=aw[:], in_=wt[:], func=ACT.Abs,
                             accum_out=rs[:, rc:rc + 1])
                c.SE.inc(s.last)
            se_a = c.SE.v

        @c.block.vector
        def _(v):
            v.wait_ge(c.SE.h, se_a)
            v.reduce_sum(out=aw[:, 0:1], in_=rs[:, 0:nR], axis=AXX)
            c.VE.inc(v.last)
        ve_r = c.VE.v

        @c.block.tensor
        def _(t):
            t.wait_ge(c.VE.h, ve_r)
            t.matmul(psG[:, :], c.ones_f[:], aw[:, 0:1], start=True, stop=True)
            c.PE.inc(t.last)
        pe_g = c.PE.v

        @c.block.scalar
        def _(s):
            s.wait_ge(c.PE.h, pe_g)
            s.activation(out=g1[0:1, 0:1], in_=psG[:, :], func=ACT.Identity,
                         scale=1.0 / (R * CF), bias=c.cc[0:1, 0:1])
            s.activation(out=g1[0:1, 1:2], in_=g1[0:1, 0:1], func=ACT.Identity,
                         scale=0.5)
            c.SE.inc(s.last)
        se_g = c.SE.v

        @c.block.sync
        def _(e):
            e.wait_ge(c.SE.h, se_g)
            c.LD.dma(e.dma_start(out=c.d["bc_scr"][0:1, 0:2], in_=g1[0:1, :]))
            e.wait_ge(c.LD.h, c.LD.v)
            c.LD.dma(e.dma_start(out=thr[:], in_=row_ap(c.d["bc_scr"], TG, 0, 1, off=1)))
        ld_t = c.LD.v

        st_q = 0
        for rc in range(nR):
            @c.block.sync
            def _(e):
                e.wait_ge(c.LD.h, ld_t)
                if st_q:
                    e.wait_ge(c.LD.h, st_q)
                c.LD.dma(e.dma_start(out=wt[:],
                                     in_=bass.AP(tap(w_dram), rc * 128 * CF,
                                                 [[CF, 128], [1, CF]])))
            ld_w = c.LD.v

            @c.block.scalar
            def _(s):
                s.wait_ge(c.LD.h, ld_w)
                s.activation(out=aw[:], in_=wt[:], func=ACT.Abs)
                s.sign(out=sg[:], in_=wt[:])
                c.SE.inc(s.last)
            se_q = c.SE.v

            @c.block.vector
            def _(v):
                v.wait_ge(c.SE.h, se_q)
                v.tensor_scalar(out=aw[:], in0=aw[:], scalar1=thr[:, 0:1],
                                scalar2=None, op0=ALU.is_gt)
                v.tensor_tensor(out=qb[:], in0=aw[:], in1=sg[:], op=ALU.mult)
                c.VE.inc(v.last)
            ve_q = c.VE.v

            @c.block.sync
            def _(e):
                e.wait_ge(c.VE.h, ve_q)
                c.LD.dma(e.dma_start(out=bass.AP(tap(wq_dram), rc * 128 * CF,
                                                 [[CF, 128], [1, CF]]),
                                     in_=qb[:]))
            st_q = c.LD.v
        barrier(c)


def phase_l1_ffn(c):
    emit_absmean_quant(c, c.d["l1_wfc_t"], c.d["wq1_sp"], 1024, 4096)
    emit_absmean_quant(c, c.d["l1_wpr_t"], c.d["wq2_sp"], 4096, 1024)
    phase_proj(c, c.d["xn_sp"], c.d["wq1_sp"], c.d["h_sp"], D, 4096,
               act="gelu", out_dt=F32)

    JC = 32
    with ExitStack() as es:
        c.es = es
        wq2sb = sb(c, "lf_w2", [128, JC, 1024], BF16)
        lnw = sb(c, "lf_lw", [128, JC], F32)
        lnb = sb(c, "lf_lb", [128, JC], F32)
        ht = sb(c, "lf_h", [128, JC, TG], F32)
        sq = [sb(c, "lf_sq", [128, TG], F32) for _ in range(2)]
        hln = sb(c, "lf_hl", [128, JC, TG], BF16)
        ab = sb(c, "lf_ab", [1, 8, TG], F32)
        abb = sb(c, "lf_abb", [128, 2, TG], F32)
        ot = [sb(c, "lf_o", [128, TG], F32) for _ in range(2)]
        stats = pst(c, "lf_st", [64, TG], F32)
        psA = [pst(c, "lf_ps", [128, TG], F32) for _ in range(2)]

        @c.block.gpsimd
        def _(e):
            wap = bass.AP(tap(c.d["wq2_sp"]), 0, [[1024, 128], [128 * 1024, JC], [1, 1024]])
            c.GD.dma(e.dma_start(out=wq2sb[:], in_=wap))
            c.GD.dma(e.dma_start(out=lnw[:], in_=c.d["l1_lnw"]))
            c.GD.dma(e.dma_start(out=lnb[:], in_=c.d["l1_lnb"]))
        gd_w = c.GD.v

        prev_tg_pe = 0
        prev_ve = 0
        prev_se = 0
        st_o = {}
        for tg in range(NTG):
            t0 = tg * TG

            @c.block.sync
            def _(e):
                if prev_tg_pe:
                    e.wait_ge(c.PE.h, prev_tg_pe)
                if prev_se:
                    e.wait_ge(c.SE.h, prev_se)
                c.LD.dma(e.dma_start(out=ht[:],
                                     in_=fm_ap(c.d["h_sp"], t0, TG, nchunk=JC)))
            ld_h = c.LD.v

            @c.block.tensor
            def _(t):
                t.wait_ge(c.LD.h, ld_h)
                for jc in range(JC):
                    t.matmul(stats[0:1, :], c.ones_f[:], ht[:, jc, :],
                             start=(jc == 0), stop=(jc == JC - 1))
                c.PE.inc(t.last)
            pe_s1 = c.PE.v

            ve_sq = {}
            pe_s2 = pe_s1
            for jc in range(JC):
                @c.block.vector
                def _(v):
                    if jc == 0:
                        v.wait_ge(c.LD.h, ld_h)
                    if jc >= 2:
                        v.wait_ge(c.PE.h, pe_s2 - (JC - 1 - (jc - 2)) if False else pe_sq[jc - 2])
                    v.tensor_tensor(out=sq[jc % 2][:], in0=ht[:, jc, :],
                                    in1=ht[:, jc, :], op=ALU.mult)
                    c.VE.inc(v.last)
                ve_sq[jc] = c.VE.v

                @c.block.tensor
                def _(t):
                    t.wait_ge(c.VE.h, ve_sq[jc])
                    t.matmul(stats[32:33, :], c.ones_f[:], sq[jc % 2][:],
                             start=(jc == 0), stop=(jc == JC - 1))
                    c.PE.inc(t.last)
                if jc == 0:
                    pe_sq = {}
                pe_sq[jc] = c.PE.v
            pe_s2 = c.PE.v

            @c.block.vector
            def _(e):
                e.wait_ge(c.PE.h, pe_s2)
                e.tensor_scalar(out=ab[0:1, 0, :], in0=stats[0:1, :], scalar1=1.0 / 4096,
                                scalar2=None, op0=ALU.mult)
                e.tensor_scalar(out=ab[0:1, 1, :], in0=stats[32:33, :], scalar1=1.0 / 4096,
                                scalar2=None, op0=ALU.mult)
                e.tensor_tensor(out=ab[0:1, 2, :], in0=ab[0:1, 0, :], in1=ab[0:1, 0, :], op=ALU.mult)
                e.tensor_tensor(out=ab[0:1, 2, :], in0=ab[0:1, 1, :], in1=ab[0:1, 2, :], op=ALU.subtract)
                e.tensor_scalar(out=ab[0:1, 2, :], in0=ab[0:1, 2, :], scalar1=1e-5,
                                scalar2=None, op0=ALU.add)
                e.reciprocal(out=ab[0:1, 2, :], in_=ab[0:1, 2, :])
                c.VE.inc(e.last)
            ve_var = c.VE.v

            @c.block.scalar
            def _(e):
                e.wait_ge(c.VE.h, ve_var)
                e.activation(out=ab[0:1, 3, :], in_=ab[0:1, 2, :], func=ACT.Sqrt)
                c.SE.inc(e.last)
            se_sq = c.SE.v

            @c.block.vector
            def _(e):
                e.wait_ge(c.SE.h, se_sq)
                e.tensor_tensor(out=ab[0:1, 4, :], in0=ab[0:1, 0, :], in1=ab[0:1, 3, :], op=ALU.mult)
                e.tensor_tensor(out=ab[0:1, 4, :], in0=ab[0:1, 7, :], in1=ab[0:1, 4, :], op=ALU.subtract) if False else None
                e.tensor_scalar(out=ab[0:1, 5, :], in0=ab[0:1, 4, :], scalar1=0.0,
                                scalar2=None, op0=ALU.mult)
                e.tensor_tensor(out=ab[0:1, 4, :], in0=ab[0:1, 5, :], in1=ab[0:1, 4, :], op=ALU.subtract)
                c.VE.inc(e.last)
            ve_b = c.VE.v

            @c.block.gpsimd
            def _(e):
                e.wait_ge(c.VE.h, ve_b)
                c.GD.dma(e.dma_start(out=c.d["bc_scr"][2:4, :], in_=ab[0:1, 3:5, :]))
                e.wait_ge(c.GD.h, c.GD.v)
                c.GD.dma(e.dma_start(out=abb[:, 0, :], in_=row_ap(c.d["bc_scr"], TG, 2, TG)))
                c.GD.dma(e.dma_start(out=abb[:, 1, :], in_=row_ap(c.d["bc_scr"], TG, 3, TG)))
            gd_ab = c.GD.v

            se_ap = {}
            for jc in range(JC):
                @c.block.vector
                def _(v):
                    if jc == 0:
                        v.wait_ge(c.GD.h, gd_ab)
                        if tg == 0:
                            v.wait_ge(c.GD.h, gd_w)
                    if jc >= 2:
                        v.wait_ge(c.SE.h, se_ap[jc - 2])
                    a_b = bass.AP(abb, 0, [[2 * TG, 128], [1, TG]])
                    b_b = bass.AP(abb, TG, [[2 * TG, 128], [1, TG]])
                    v.tensor_tensor(out=sq[jc % 2][:], in0=ht[:, jc, :], in1=a_b, op=ALU.mult)
                    v.tensor_tensor(out=sq[jc % 2][:], in0=sq[jc % 2][:], in1=b_b, op=ALU.add)
                    c.VE.inc(v.last)
                ve_ap = c.VE.v

                @c.block.scalar
                def _(s):
                    s.wait_ge(c.VE.h, ve_ap)
                    s.activation(out=hln[:, jc, :], in_=sq[jc % 2][:], func=ACT.Identity,
                                 scale=lnw[:, jc:jc + 1], bias=lnb[:, jc:jc + 1])
                    c.SE.inc(s.last)
                se_ap[jc] = c.SE.v
            se_apl = c.SE.v

            se_ev = {}
            for o_i in range(DC):
                @c.block.tensor
                def _(t):
                    if o_i == 0:
                        t.wait_ge(c.SE.h, se_apl)
                    if o_i >= 2:
                        t.wait_ge(c.SE.h, se_ev[o_i - 2])
                    for jc in range(JC):
                        t.matmul(psA[o_i % 2][:, :], wq2sb[:, jc, o_i * 128:(o_i + 1) * 128],
                                 hln[:, jc, :], start=(jc == 0), stop=(jc == JC - 1))
                    c.PE.inc(t.last)
                pe_o = c.PE.v
                if o_i == DC - 1:
                    prev_tg_pe = pe_o

                @c.block.scalar
                def _(s):
                    s.wait_ge(c.PE.h, pe_o)
                    lk = st_o.get((tg, o_i - 2)) or (st_o.get((tg - 1, DC + o_i - 2))
                                                     if tg > 0 else None)
                    if lk:
                        s.wait_ge(c.LD.h, lk)
                    s.activation(out=ot[o_i % 2][:], in_=psA[o_i % 2][:, :], func=ACT.Silu)
                    c.SE.inc(s.last)
                se_ev[o_i] = c.SE.v

                @c.block.sync
                def _(e):
                    e.wait_ge(c.SE.h, se_ev[o_i])
                    dap = bass.AP(tap(c.d["g_sp"]), o_i * 128 * T + t0, [[T, 128], [1, TG]])
                    c.LD.dma(e.dma_start(out=dap, in_=ot[o_i % 2][:]))
                st_o[(tg, o_i)] = c.LD.v
            prev_se = c.SE.v
        barrier(c)



def phase_l2_mamba(c):
    phase_proj(c, c.d["xn_sp"], c.d["l2_ipw_t"], c.d["h_sp"], D, 4096,
               act="copy", out_dt=F32)

    with ExitStack() as es:
        c.es = es
        cw = sb(c, "cv_w", [128, 16, 4], F32)
        cb = sb(c, "cv_b", [128, 16], F32)
        xt = sb(c, "cv_x", [128, 2048 + 4], F32)
        acc = sb(c, "cv_a", [128, 2048], F32)
        xc = sb(c, "cv_c", [128, 2048], F32)

        @c.block.sync
        def _(e):
            c.LD.dma(e.dma_start(out=cw[:], in_=c.d["l2_convw"]))
            c.LD.dma(e.dma_start(out=cb[:], in_=c.d["l2_convb"]))
        ld_c = c.LD.v

        se_z = 0
        gd_pr = 0
        st_pr = 0
        for seq in range(NS):
            s0 = seq * LSEQ
            for cc in range(16):
                @c.block.scalar
                def _(s):
                    if st_pr:
                        s.wait_ge(c.LD.h, st_pr)
                    if gd_pr:
                        s.wait_ge(c.GD.h, gd_pr)
                    s.memzero(xt[:, 0:3])
                    c.SE.inc(s.last)
                se_z = c.SE.v

                @c.block.sync
                def _(e):
                    e.wait_ge(c.SE.h, se_z)
                    src_ap = bass.AP(tap(c.d["h_sp"]), (cc * 128) * T + s0,
                                     [[T, 128], [1, 2048]])
                    c.LD.dma(e.dma_start(out=xt[:, 3:2051], in_=src_ap))
                ld_x = c.LD.v

                @c.block.scalar
                def _(s):
                    s.wait_ge(c.LD.h, max(ld_x, ld_c))
                    s.activation(out=acc[:], in_=xt[:, 0:2048], func=ACT.Identity,
                                 scale=cw[:, cc, 0:1], bias=cb[:, cc:cc + 1])
                    c.SE.inc(s.last)
                se_c0 = c.SE.v

                @c.block.vector
                def _(v):
                    v.wait_ge(c.SE.h, se_c0)
                    for j in range(1, 4):
                        v.scalar_tensor_tensor(out=acc[:], in0=xt[:, j:2048 + j],
                                               scalar=cw[:, cc, j:j + 1], in1=acc[:],
                                               op0=ALU.mult, op1=ALU.add)
                    c.VE.inc(v.last)
                ve_c = c.VE.v

                @c.block.scalar
                def _(s):
                    s.wait_ge(c.VE.h, ve_c)
                    s.activation(out=xc[:], in_=acc[:], func=ACT.Silu)
                    c.SE.inc(s.last)
                se_s = c.SE.v

                @c.block.sync
                def _(e):
                    e.wait_ge(c.SE.h, se_s)
                    dap = bass.AP(tap(c.d["xc_sp"]), (cc * 128) * T + s0,
                                  [[T, 128], [1, 2048]])
                    c.LD.dma(e.dma_start(out=dap, in_=xc[:]))
                st_pr = c.LD.v

                @c.block.gpsimd
                def _(g):
                    g.wait_ge(c.SE.h, se_s)
                    dap = bass.AP(tap(c.d["xcb_sp"]), (cc * 128) * T + s0,
                                  [[T, 128], [1, 2048]])
                    c.GD.dma(g.dma_start(out=dap, in_=xc[:]))
                gd_pr = c.GD.v
        barrier(c)

    with ExitStack() as es:
        c.es = es
        xpw = sb(c, "xd_w", [128, 16, 96], BF16)
        xct = [sb(c, "xd_x", [128, 16, TG], BF16) for _ in range(2)]
        ot = [sb(c, "xd_o", [96, TG], F32) for _ in range(2)]
        psB = [pst(c, "xd_ps", [96, TG], F32) for _ in range(2)]

        @c.block.gpsimd
        def _(g):
            c.GD.dma(g.dma_start(out=xpw[:], in_=bass.AP(tap(c.d["l2_xpw_t"]), 0,
                                                         [[96, 128], [128 * 96, 16], [1, 96]])))
        gd_w = c.GD.v

        pe_m = {}
        st_o = {}
        for tg in range(NTG):
            t0 = tg * TG

            @c.block.sync
            def _(e):
                if pe_m.get(tg - 2):
                    e.wait_ge(c.PE.h, pe_m[tg - 2])
                c.LD.dma(e.dma_start(out=xct[tg % 2][:],
                                     in_=fm_ap(c.d["xcb_sp"], t0, TG, nchunk=16)))
            ld_x = c.LD.v

            @c.block.tensor
            def _(t):
                t.wait_ge(c.LD.h, ld_x)
                if tg == 0:
                    t.wait_ge(c.GD.h, gd_w)
                if tg >= 2:
                    t.wait_ge(c.SE.h, se_o[tg - 2])
                for ic in range(16):
                    t.matmul(psB[tg % 2][:, :], xpw[:, ic, :], xct[tg % 2][:, ic, :],
                             start=(ic == 0), stop=(ic == 15))
                c.PE.inc(t.last)
            pe_m[tg] = c.PE.v

            @c.block.scalar
            def _(s):
                s.wait_ge(c.PE.h, pe_m[tg])
                if st_o.get(tg - 2):
                    s.wait_ge(c.LD.h, st_o[tg - 2])
                s.copy(out=ot[tg % 2][:], in_=psB[tg % 2][:, :])
                c.SE.inc(s.last)
            if tg == 0:
                se_o = {}
            se_o[tg] = c.SE.v

            @c.block.sync
            def _(e):
                e.wait_ge(c.SE.h, se_o[tg])
                dap = bass.AP(tap(c.d["xdbl_sp"]), t0, [[T, 96], [1, TG]])
                c.LD.dma(e.dma_start(out=dap, in_=ot[tg % 2][:]))
            st_o[tg] = c.LD.v
        barrier(c)

    with ExitStack() as es:
        c.es = es
        k16 = sb(c, "kk_k", [16, LSEQ], F32)
        ksq = sb(c, "kk_s", [16, LSEQ], F32)
        kkr = sb(c, "kk_r", [1, LSEQ], F32)
        psK = pst(c, "kk_ps", [1, LSEQ], F32)
        for seq in range(NS):
            s0 = seq * LSEQ

            @c.block.sync
            def _(e):
                if seq:
                    e.wait_ge(c.LD.h, c.LD.v)
                c.LD.dma(e.dma_start(out=k16[:], in_=bass.AP(tap(c.d["xdbl_sp"]),
                                                             64 * T + s0, [[T, 16], [1, LSEQ]])))
            ld_k = c.LD.v

            @c.block.vector
            def _(v):
                v.wait_ge(c.LD.h, ld_k)
                v.tensor_tensor(out=ksq[:], in0=k16[:], in1=k16[:], op=ALU.mult)
                c.VE.inc(v.last)
            ve_k = c.VE.v

            @c.block.tensor
            def _(t):
                t.wait_ge(c.VE.h, ve_k)
                for j in range(4):
                    t.matmul(psK[:, j * TG:(j + 1) * TG], c.ones_f[0:16, 0:1],
                             ksq[:, j * TG:(j + 1) * TG], start=True, stop=True)
                c.PE.inc(t.last)
            pe_k = c.PE.v

            @c.block.scalar
            def _(s):
                s.wait_ge(c.PE.h, pe_k)
                s.copy(out=kkr[:], in_=psK[:, :])
                c.SE.inc(s.last)
            se_k = c.SE.v

            @c.block.sync
            def _(e):
                e.wait_ge(c.SE.h, se_k)
                c.LD.dma(e.dma_start(out=bass.AP(tap(c.d["kks_sp"]), s0, [[T, 1], [1, LSEQ]]),
                                     in_=kkr[:]))
        barrier(c)

    with ExitStack() as es:
        c.es = es
        dthw = sb(c, "dt_w", [64, 16, 128], BF16)
        dthb = sb(c, "dt_b", [128, 16], F32)
        dtl = [sb(c, "dt_l", [64, TG], BF16) for _ in range(2)]
        kkb = sb(c, "dt_kk", [128, TG], F32)
        s1 = sb(c, "dt_s1", [128, TG], F32)
        den = sb(c, "dt_dn", [128, TG], F32)
        dtv = sb(c, "dt_v", [128, TG], F32)
        xctl = sb(c, "dt_xc", [128, TG], F32)
        xdt = sb(c, "dt_xd", [128, TG], BF16)
        psD = [pst(c, "dt_ps", [128, TG], F32) for _ in range(2)]

        @c.block.gpsimd
        def _(g):
            c.GD.dma(g.dma_start(out=dthw[:], in_=bass.AP(tap(c.d["l2_dthw_t"]), 0,
                                                          [[2048, 64], [128, 16], [1, 128]])))
            c.GD.dma(g.dma_start(out=dthb[:], in_=c.d["l2_dthb"]))
        gd_w = c.GD.v

        st_v = {}
        for tg in range(NTG):
            t0 = tg * TG

            @c.block.gpsimd
            def _(g):
                if tg >= 2:
                    g.wait_ge(c.PE.h, pe_d[(tg - 2, 15)])
                g.dma_start(out=dtl[tg % 2][:], in_=bass.AP(tap(c.d["xdbl_sp"]), t0,
                                                            [[T, 64], [1, TG]]))
                c.GD.inc(g.last, 16)
            gd_l = c.GD.v

            @c.block.sync
            def _(e):
                if tg >= 1:
                    e.wait_ge(c.VE.h, ve_d[(tg - 1, 15)])
                c.LD.dma(e.dma_start(out=kkb[:], in_=row_ap(c.d["kks_sp"], T, 0, TG, off=t0)))
            ld_kk = c.LD.v

            for dtc in range(16):
                @c.block.tensor
                def _(t):
                    if tg == 0 and dtc == 0:
                        t.wait_ge(c.GD.h, gd_w)
                    if dtc == 0:
                        t.wait_ge(c.GD.h, gd_l)
                    if dtc >= 2:
                        t.wait_ge(c.SE.h, se_d[(tg, dtc - 2)])
                    elif tg >= 1 and dtc < 2:
                        t.wait_ge(c.SE.h, se_d[(tg - 1, 14 + dtc)])
                    t.matmul(psD[dtc % 2][:, :], dthw[:, dtc, :], dtl[tg % 2][:],
                             start=True, stop=True)
                    c.PE.inc(t.last)
                if tg == 0 and dtc == 0:
                    pe_d, se_d, ve_d = {}, {}, {}
                pe_d[(tg, dtc)] = c.PE.v

                @c.block.scalar
                def _(s):
                    s.wait_ge(c.PE.h, pe_d[(tg, dtc)])
                    if dtc == 0:
                        s.wait_ge(c.LD.h, ld_kk)
                    if dtc >= 1:
                        s.wait_ge(c.VE.h, ve_d[(tg, dtc - 1)])
                    s.activation(out=s1[:], in_=psD[dtc % 2][:, :], func=ACT.Sigmoid,
                                 bias=dthb[:, dtc:dtc + 1])
                    c.SE.inc(s.last)
                se_d[(tg, dtc)] = c.SE.v

                @c.block.sync
                def _(e):
                    lk = st_v.get((tg, dtc - 1)) or (st_v.get((tg - 1, 15)) if tg else None)
                    if lk:
                        e.wait_ge(c.LD.h, lk)
                    src_ap = bass.AP(tap(c.d["xc_sp"]), (dtc * 128) * T + t0,
                                     [[T, 128], [1, TG]])
                    c.LD.dma(e.dma_start(out=xctl[:], in_=src_ap))
                ld_xc = c.LD.v

                @c.block.vector
                def _(v):
                    v.wait_ge(c.SE.h, se_d[(tg, dtc)])
                    v.wait_ge(c.LD.h, ld_xc)
                    v.tensor_tensor(out=den[:], in0=s1[:], in1=kkb[:], op=ALU.mult)
                    v.tensor_scalar(out=den[:], in0=den[:], scalar1=1.0, scalar2=None,
                                    op0=ALU.add)
                    v.reciprocal(out=den[:], in_=den[:])
                    v.tensor_tensor(out=dtv[:], in0=s1[:], in1=den[:], op=ALU.mult)
                    v.tensor_tensor(out=xdt[:], in0=dtv[:], in1=xctl[:], op=ALU.mult)
                    c.VE.inc(v.last)
                ve_d[(tg, dtc)] = c.VE.v

                @c.block.sync
                def _(e):
                    e.wait_ge(c.VE.h, ve_d[(tg, dtc)])
                    c.LD.dma(e.dma_start(out=bass.AP(tap(c.d["dtv_sp"]), (dtc * 128) * T + t0,
                                                     [[T, 128], [1, TG]]), in_=dtv[:]))
                    c.LD.dma(e.dma_start(out=bass.AP(tap(c.d["xdt_sp"]), (dtc * 128) * T + t0,
                                                     [[T, 128], [1, TG]]), in_=xdt[:]))
                st_v[(tg, dtc)] = c.LD.v
        barrier(c)

    with ExitStack() as es:
        c.es = es
        kb = [sb(c, "sc_kb", [128, LSEQ], BF16) for _ in range(16)]
        qb = [sb(c, "sc_qb", [128, LSEQ], BF16) for _ in range(16)]
        Dcol = sb(c, "sc_D", [128, 16], F32)
        dtvt = sb(c, "sc_dv", [128, LSEQ], F32)
        xdtb = sb(c, "sc_xd", [128, LSEQ], BF16)
        aux = sb(c, "sc_ax", [128, LSEQ], F32)
        y = sb(c, "sc_y", [128, LSEQ], F32)
        u = sb(c, "sc_u", [128, LSEQ], F32)
        bt = sb(c, "sc_b", [128, LSEQ], F32)
        st = sb(c, "sc_s", [128, LSEQ], F32)
        stb = sb(c, "sc_sb", [128, LSEQ], BF16)
        ymb = sb(c, "sc_ym", [128, LSEQ], BF16)

        @c.block.sync
        def _(e):
            c.LD.dma(e.dma_start(out=Dcol[:], in_=c.d["l2_D"]))
        ld_D = c.LD.v

        ones_b = bass.AP(c.ones_f, 0, [[1, 128], [0, LSEQ]])
        st_y = 0
        for seq in range(NS):
            s0 = seq * LSEQ

            @c.block.gpsimd
            def _(g):
                if seq:
                    g.wait_ge(c.VE.h, c.VE.v)
                for n in range(16):
                    g.dma_start(out=kb[n][:], in_=row_ap(c.d["xdbl_sp"], T, 64 + n,
                                                         LSEQ, off=s0))
                    c.GD.inc(g.last, 16)
                    g.dma_start(out=qb[n][:], in_=row_ap(c.d["xdbl_sp"], T, 80 + n,
                                                         LSEQ, off=s0))
                    c.GD.inc(g.last, 16)
            gd_b = c.GD.v

            for dc in range(16):
                @c.block.sync
                def _(e):
                    if dc or seq:
                        e.wait_ge(c.VE.h, c.VE.v)
                    c.LD.dma(e.dma_start(out=dtvt[:], in_=bass.AP(tap(c.d["dtv_sp"]),
                                                                  (dc * 128) * T + s0, [[T, 128], [1, LSEQ]])))
                    c.LD.dma(e.dma_start(out=xdtb[:], in_=bass.AP(tap(c.d["xdt_sp"]),
                                                                  (dc * 128) * T + s0, [[T, 128], [1, LSEQ]])))
                    c.LD.dma(e.dma_start(out=aux[:], in_=bass.AP(tap(c.d["xc_sp"]),
                                                                 (dc * 128) * T + s0, [[T, 128], [1, LSEQ]])))
                ld_t = c.LD.v

                @c.block.scalar
                def _(s):
                    s.wait_ge(c.LD.h, max(ld_t, ld_D))
                    s.activation(out=y[:], in_=aux[:], func=ACT.Identity,
                                 scale=Dcol[:, dc:dc + 1])
                    c.SE.inc(s.last)
                se_y0 = c.SE.v

                se_prev = se_y0
                for n in range(16):
                    @c.block.vector
                    def _(v):
                        if n == 0:
                            v.wait_ge(c.GD.h, gd_b)
                            v.wait_ge(c.SE.h, se_y0)
                        else:
                            v.wait_ge(c.SE.h, se_prev)
                        v.tensor_tensor(out=u[:], in0=kb[n][:], in1=kb[n][:], op=ALU.mult)
                        v.tensor_tensor(out=u[:], in0=u[:], in1=dtvt[:], op=ALU.mult)
                        v.tensor_tensor(out=u[:], in0=ones_b, in1=u[:], op=ALU.subtract)
                        v.tensor_tensor(out=bt[:], in0=xdtb[:], in1=kb[n][:], op=ALU.mult)
                        v.tensor_tensor_scan(out=st[:], data0=u[:], data1=bt[:],
                                             initial=c.zcol[:], op0=ALU.mult, op1=ALU.add)
                        c.VE.inc(v.last)
                    ve_s = c.VE.v

                    @c.block.scalar
                    def _(s):
                        s.wait_ge(c.VE.h, ve_s)
                        s.copy(out=stb[:], in_=st[:])
                        c.SE.inc(s.last)
                    se_c = c.SE.v
                    se_prev = se_c

                    @c.block.vector
                    def _(v):
                        v.wait_ge(c.SE.h, se_c)
                        v.tensor_tensor(out=u[:], in0=stb[:], in1=qb[n][:], op=ALU.mult)
                        v.tensor_tensor(out=y[:], in0=y[:], in1=u[:], op=ALU.add)
                        c.VE.inc(v.last)

                @c.block.sync
                def _(e):
                    e.wait_ge(c.VE.h, c.VE.v)
                    c.LD.dma(e.dma_start(out=aux[:], in_=bass.AP(tap(c.d["h_sp"]),
                                                                 (2048 + dc * 128) * T + s0,
                                                                 [[T, 128], [1, LSEQ]])))
                ld_z = c.LD.v

                @c.block.scalar
                def _(s):
                    s.wait_ge(c.LD.h, ld_z)
                    s.activation(out=aux[:], in_=aux[:], func=ACT.Silu)
                    c.SE.inc(s.last)
                se_z = c.SE.v

                @c.block.vector
                def _(v):
                    v.wait_ge(c.SE.h, se_z)
                    if st_y:
                        v.wait_ge(c.LD.h, st_y)
                    v.tensor_tensor(out=ymb[:], in0=y[:], in1=aux[:], op=ALU.mult)
                    c.VE.inc(v.last)
                ve_g = c.VE.v

                @c.block.sync
                def _(e):
                    e.wait_ge(c.VE.h, ve_g)
                    c.LD.dma(e.dma_start(out=bass.AP(tap(c.d["yz_sp"]), (dc * 128) * T + s0,
                                                     [[T, 128], [1, LSEQ]]), in_=ymb[:]))
                st_y = c.LD.v
        barrier(c)

    phase_proj(c, c.d["yz_sp"], c.d["l2_opw_t"], c.d["s_sp"], 2048, D,
               act="copy", out_dt=F32)



def phase_l2_moeattn(c):
    with ExitStack() as es:
        c.es = es
        sim = sb(c, "mg_sim", [128, 8, 16], F32)
        ssq = sb(c, "mg_ssq", [128, 8, 16], F32)
        simn = sb(c, "mg_sn", [128, 8, 16], BF16)
        nrm = sb(c, "mg_nr", [1, 16], F32)
        nrmb = sb(c, "mg_nb", [128, 16], F32)
        gcol = sb(c, "mg_gc", [16, 2], F32)
        xnt = [sb(c, "mg_x", [128, DC, TG], BF16) for _ in range(2)]
        xsqb = sb(c, "mg_xq", [128, DC, TG], BF16)
        rnr = sb(c, "mg_rn", [1, TG], F32)
        logit = sb(c, "mg_lg", [16, T], F32)
        rnb = sb(c, "mg_rb", [16, TG], F32)
        ltm = sb(c, "mg_lt", [128, 32, 16], F32)
        gat = sb(c, "mg_ga", [128, 32, 16], F32)
        msk = sb(c, "mg_mk", [128, 32, 16], F32)
        wrk = sb(c, "mg_wk", [128, 32, 16], F32)
        sc6 = sb(c, "mg_s6", [128, 32, 6], F32)
        rww = sb(c, "mg_rw", [16, T], F32)
        psN = pst(c, "mg_pn", [1, 16], F32)
        psR = pst(c, "mg_pr", [1, TG], F32)
        psL = [pst(c, "mg_pl", [16, TG], F32) for _ in range(2)]
        psG = pst(c, "mg_pg", [128, 512], F32)
        psW = pst(c, "mg_pw", [16, 1024], F32)

        @c.block.sync
        def _(e):
            c.LD.dma(e.dma_start(out=sim[:], in_=bass.AP(tap(c.d["l2_sim"]), 0,
                                                         [[16, 128], [128 * 16, 8], [1, 16]])))
            c.LD.dma(e.dma_start(out=gcol[:, 0:1], in_=bass.AP(tap(c.d["l2_gates"]), 0,
                                                               [[1, 16], [1, 1]])))
        ld_s = c.LD.v

        @c.block.vector
        def _(v):
            v.wait_ge(c.LD.h, ld_s)
            v.tensor_tensor(out=ssq[:], in0=sim[:], in1=sim[:], op=ALU.mult)
            c.VE.inc(v.last)
        ve_sq = c.VE.v

        @c.block.tensor
        def _(t):
            t.wait_ge(c.VE.h, ve_sq)
            for dc in range(8):
                t.matmul(psN[:, :], c.ones_f[:], ssq[:, dc, :],
                         start=(dc == 0), stop=(dc == 7))
            c.PE.inc(t.last)
        pe_n = c.PE.v

        @c.block.vector
        def _(v):
            v.wait_ge(c.PE.h, pe_n)
            v.reciprocal(out=nrm[:], in_=psN[:, :])
            c.VE.inc(v.last)
        ve_nr = c.VE.v

        @c.block.scalar
        def _(s):
            s.wait_ge(c.VE.h, ve_nr)
            s.activation(out=nrm[:], in_=nrm[:], func=ACT.Sqrt)
            s.activation(out=gcol[:, 1:2], in_=gcol[:, 0:1], func=ACT.Sigmoid)
            c.SE.inc(s.last)
        se_n = c.SE.v

        @c.block.sync
        def _(e):
            e.wait_ge(c.SE.h, se_n)
            c.LD.dma(e.dma_start(out=c.d["bc_scr"][4:5, 0:16], in_=nrm[:]))
            e.wait_ge(c.LD.h, c.LD.v)
            c.LD.dma(e.dma_start(out=nrmb[:], in_=row_ap(c.d["bc_scr"], TG, 4, 16)))
        ld_nb = c.LD.v

        @c.block.vector
        def _(v):
            v.wait_ge(c.LD.h, ld_nb)
            nb = bass.AP(nrmb, 0, [[16, 128], [0, 8], [1, 16]])
            v.tensor_tensor(out=simn[:], in0=sim[:], in1=nb, op=ALU.mult)
            c.VE.inc(v.last)
        ve_sn = c.VE.v

        se_rn = {}
        ve_lg = {}
        for tg in range(NTG):
            t0 = tg * TG

            @c.block.sync
            def _(e):
                if tg >= 2:
                    e.wait_ge(c.PE.h, pe_lg[tg - 2])
                c.LD.dma(e.dma_start(out=xnt[tg % 2][:], in_=fm_ap(c.d["xn_sp"], t0, TG)))
            ld_x = c.LD.v

            @c.block.vector
            def _(v):
                v.wait_ge(c.LD.h, ld_x)
                if tg >= 1:
                    v.wait_ge(c.PE.h, pe_rn[tg - 1])
                v.tensor_tensor(out=xsqb[:], in0=xnt[tg % 2][:], in1=xnt[tg % 2][:],
                                op=ALU.mult)
                c.VE.inc(v.last)
            ve_x = c.VE.v

            @c.block.tensor
            def _(t):
                t.wait_ge(c.VE.h, ve_x)
                if tg >= 1:
                    t.wait_ge(c.SE.h, se_rn[tg - 1])
                for dc in range(DC):
                    t.matmul(psR[:, :], c.ones_bf[:], xsqb[:, dc, :],
                             start=(dc == 0), stop=(dc == DC - 1))
                c.PE.inc(t.last)
            if tg == 0:
                pe_rn, pe_lg = {}, {}
            pe_rn[tg] = c.PE.v

            @c.block.vector
            def _(v):
                v.wait_ge(c.PE.h, pe_rn[tg])
                v.reciprocal(out=rnr[:], in_=psR[:, :])
                c.VE.inc(v.last)
            ve_rn = c.VE.v

            @c.block.scalar
            def _(s):
                s.wait_ge(c.VE.h, ve_rn)
                s.activation(out=rnr[:], in_=rnr[:], func=ACT.Sqrt)
                c.SE.inc(s.last)
            se_rn[tg] = c.SE.v

            @c.block.sync
            def _(e):
                e.wait_ge(c.SE.h, se_rn[tg])
                c.LD.dma(e.dma_start(out=c.d["bc2_scr"][0:1, 0:TG], in_=rnr[:]))
                e.wait_ge(c.LD.h, c.LD.v)
                c.LD.dma(e.dma_start(out=rnb[:], in_=row_ap(c.d["bc2_scr"], 2048, 0, TG,
                                                            repl=16)))
            ld_rb = c.LD.v

            @c.block.tensor
            def _(t):
                if tg == 0:
                    t.wait_ge(c.VE.h, ve_sn)
                if tg >= 1:
                    t.wait_ge(c.VE.h, ve_lg[tg - 1])
                for dc in range(DC):
                    t.matmul(psL[tg % 2][:, :], simn[:, dc, :], xnt[tg % 2][:, dc, :],
                             start=(dc == 0), stop=(dc == DC - 1))
                c.PE.inc(t.last)
            pe_lg[tg] = c.PE.v

            @c.block.vector
            def _(v):
                v.wait_ge(c.PE.h, pe_lg[tg])
                v.wait_ge(c.LD.h, ld_rb)
                v.tensor_tensor(out=logit[:, t0:t0 + TG], in0=psL[tg % 2][:, :],
                                in1=rnb[:], op=ALU.mult)
                v.tensor_scalar(out=logit[:, t0:t0 + TG], in0=logit[:, t0:t0 + TG],
                                scalar1=gcol[:, 1:2], scalar2=None, op0=ALU.subtract)
                c.VE.inc(v.last)
            ve_lg[tg] = c.VE.v
        ve_lgl = c.VE.v

        @c.block.tensor
        def _(t):
            t.wait_ge(c.VE.h, ve_lgl)
            for blk in range(32):
                t.transpose(psG[:, blk * 16:(blk + 1) * 16],
                            logit[:, blk * 128:(blk + 1) * 128], c.id_f[0:16, 0:16])
            c.PE.inc(t.last)
        pe_t = c.PE.v

        @c.block.vector
        def _(v):
            v.wait_ge(c.PE.h, pe_t)
            v.tensor_copy(out=ltm[:], in_=psG[:, :])
            v.tensor_scalar(out=gat[:], in0=ltm[:], scalar1=0.0, scalar2=None,
                            op0=ALU.max)
            v.tensor_scalar(out=msk[:], in0=ltm[:], scalar1=0.0, scalar2=None,
                            op0=ALU.is_gt)
            v.reduce_max(out=sc6[:, :, 0:1], in_=ltm[:], axis=AXX)
            m1b = bass.AP(sc6, 0, [[32 * 6, 128], [6, 32], [0, 16]])
            v.tensor_tensor(out=wrk[:], in0=ltm[:], in1=m1b, op=ALU.is_equal)
            v.scalar_tensor_tensor(out=wrk[:], in0=wrk[:], scalar=-1e30, in1=ltm[:],
                                   op0=ALU.mult, op1=ALU.add)
            v.reduce_max(out=sc6[:, :, 1:2], in_=wrk[:], axis=AXX)
            m2b = bass.AP(sc6, 1, [[32 * 6, 128], [6, 32], [0, 16]])
            v.tensor_tensor(out=wrk[:], in0=ltm[:], in1=m2b, op=ALU.is_ge)
            v.tensor_scalar(out=sc6[:, :, 2:3], in0=sc6[:, :, 0:1], scalar1=0.0,
                            scalar2=None, op0=ALU.is_le)
            inb = bass.AP(sc6, 2, [[32 * 6, 128], [6, 32], [0, 16]])
            v.tensor_tensor(out=wrk[:], in0=wrk[:], in1=inb, op=ALU.mult)
            v.tensor_tensor(out=msk[:], in0=msk[:], in1=wrk[:], op=ALU.max)
            c.VE.inc(v.last)
        ve_g1 = c.VE.v

        @c.block.scalar
        def _(s):
            s.wait_ge(c.VE.h, ve_g1)
            s.activation(out=gat[:], in_=gat[:], func=ACT.Exp)
            c.SE.inc(s.last)
        se_e = c.SE.v

        @c.block.vector
        def _(v):
            v.wait_ge(c.SE.h, se_e)
            v.tensor_tensor(out=gat[:], in0=gat[:], in1=msk[:], op=ALU.mult)
            v.reduce_sum(out=sc6[:, :, 3:4], in_=gat[:], axis=AXX)
            v.reciprocal(out=sc6[:, :, 4:5], in_=sc6[:, :, 3:4])
            rsb = bass.AP(sc6, 4, [[32 * 6, 128], [6, 32], [0, 16]])
            v.tensor_tensor(out=gat[:], in0=gat[:], in1=rsb, op=ALU.mult)
            c.VE.inc(v.last)
        ve_rw = c.VE.v

        for rnd in range(4):
            @c.block.tensor
            def _(t):
                t.wait_ge(c.VE.h, ve_rw)
                if rnd:
                    t.wait_ge(c.SE.h, se_w)
                for b in range(8):
                    blk = rnd * 8 + b
                    t.transpose(psW[:, b * 128:(b + 1) * 128], gat[:, blk, :],
                                c.id_f[:])
                c.PE.inc(t.last)
            pe_w = c.PE.v

            @c.block.scalar
            def _(s):
                s.wait_ge(c.PE.h, pe_w)
                s.copy(out=rww[:, rnd * 1024:(rnd + 1) * 1024], in_=psW[:, :])
                c.SE.inc(s.last)
            se_w = c.SE.v

        @c.block.sync
        def _(e):
            e.wait_ge(c.SE.h, se_w)
            c.LD.dma(e.dma_start(out=c.d["rw_sp"], in_=rww[:]))
            e.wait_ge(c.LD.h, c.LD.v)
        barrier(c)

    with ExitStack() as es:
        c.es = es
        xnt = [sb(c, "mq_x", [128, DC, TG], BF16) for _ in range(2)]
        wqe = [sb(c, "mq_wq", [128, DC, 64], BF16) for _ in range(2)]
        wke = [sb(c, "mq_wk", [128, DC, 64], BF16) for _ in range(2)]
        wve = [sb(c, "mq_wv", [128, DC, 64], BF16) for _ in range(2)]
        rwb = sb(c, "mq_rw", [128, TG], F32)
        qac = sb(c, "mq_qa", [64, 3, TG], F32)
        tmp = sb(c, "mq_tp", [64, TG], F32)
        ob = sb(c, "mq_ob", [64, 3, TG], BF16)
        psq = pst(c, "mq_pq", [64, TG], F32)
        psk = pst(c, "mq_pk", [64, TG], F32)
        psv = pst(c, "mq_pv", [64, TG], F32)

        st_o = 0
        ve_ac = 0
        for tg in range(NTG):
            t0 = tg * TG

            @c.block.sync
            def _(e):
                if tg >= 2:
                    e.wait_ge(c.PE.h, pe_e[15])
                c.LD.dma(e.dma_start(out=xnt[tg % 2][:], in_=fm_ap(c.d["xn_sp"], t0, TG)))
            ld_x = c.LD.v

            pe_e = {}
            for e_i in range(16):
                @c.block.gpsimd
                def _(g):
                    if pe_e.get(e_i - 2) or (tg > 0 and e_i < 2):
                        g.wait_ge(c.PE.h, pe_e.get(e_i - 2) or pe_prev)
                    for wsl, nm in ((wqe, "l2_qp"), (wke, "l2_kp"), (wve, "l2_vp")):
                        src = bass.AP(tap(c.d[nm]), e_i * 1024 * 64,
                                      [[64, 128], [128 * 64, DC], [1, 64]])
                        g.dma_start(out=wsl[e_i % 2][:], in_=src)
                        c.GD.inc(g.last, 16)
                    if e_i == 0:
                        g.wait_ge(c.VE.h, ve_ac)
                        g.dma_start(out=rwb[:], in_=row_ap(c.d["rw_sp"], T, 0, TG, off=t0))
                        c.GD.inc(g.last, 16)
                gd_w = c.GD.v

                @c.block.tensor
                def _(t):
                    t.wait_ge(c.GD.h, gd_w)
                    if e_i == 0:
                        t.wait_ge(c.LD.h, ld_x)
                    if e_i >= 1:
                        t.wait_ge(c.VE.h, ve_e[e_i - 1])
                    for dc in range(DC):
                        t.matmul(psq[:, :], wqe[e_i % 2][:, dc, :], xnt[tg % 2][:, dc, :],
                                 start=(dc == 0), stop=(dc == DC - 1))
                    for dc in range(DC):
                        t.matmul(psk[:, :], wke[e_i % 2][:, dc, :], xnt[tg % 2][:, dc, :],
                                 start=(dc == 0), stop=(dc == DC - 1))
                    for dc in range(DC):
                        t.matmul(psv[:, :], wve[e_i % 2][:, dc, :], xnt[tg % 2][:, dc, :],
                                 start=(dc == 0), stop=(dc == DC - 1))
                    c.PE.inc(t.last)
                pe_e[e_i] = c.PE.v

                @c.block.gpsimd
                def _(g):
                    if e_i == 0:
                        g.wait_ge(c.GD.h, gd_w)
                        src = bass.AP(tap(c.d["rw_sp"]), (e_i + 1) * T + t0, [[0, 128], [1, TG]])

                @c.block.vector
                def _(v):
                    v.wait_ge(c.PE.h, pe_e[e_i])
                    if e_i == 0:
                        v.wait_ge(c.GD.h, gd_w)
                    rb = bass.AP(rwb, 0, [[TG, 64], [1, TG]]) if False else rwb[0:64, :]
                    for j, ps in enumerate((psq, psk, psv)):
                        v.tensor_tensor(out=tmp[:], in0=ps[:, :], in1=rwb[0:64, :],
                                        op=ALU.mult)
                        if e_i == 0:
                            v.tensor_copy(out=qac[:, j, :], in_=tmp[:])
                        else:
                            v.tensor_tensor(out=qac[:, j, :], in0=qac[:, j, :],
                                            in1=tmp[:], op=ALU.add)
                    c.VE.inc(v.last)
                if e_i == 0:
                    ve_e = {}
                ve_e[e_i] = c.VE.v

                @c.block.gpsimd
                def _(g):
                    g.wait_ge(c.PE.h, pe_e[e_i])
                    if e_i < 15:
                        g.wait_ge(c.VE.h, ve_e[e_i])
                        src = bass.AP(tap(c.d["rw_sp"]), (e_i + 1) * T + t0,
                                      [[0, 128], [1, TG]])
                        g.dma_start(out=rwb[:], in_=src)
                        c.GD.inc(g.last, 16)
            pe_prev = c.PE.v

            @c.block.scalar
            def _(s):
                s.wait_ge(c.VE.h, ve_e[15])
                if st_o:
                    s.wait_ge(c.LD.h, st_o)
                s.activation(out=ob[:, 0, :], in_=qac[:, 0, :], func=ACT.Identity,
                             scale=0.125)
                s.copy(out=ob[:, 1, :], in_=qac[:, 1, :])
                s.copy(out=ob[:, 2, :], in_=qac[:, 2, :])
                c.SE.inc(s.last)
            se_o = c.SE.v
            ve_ac = ve_e[15]

            @c.block.sync
            def _(e):
                e.wait_ge(c.SE.h, se_o)
                for j, nm in enumerate(("q2_sp", "k2_sp", "v2_sp")):
                    c.LD.dma(e.dma_start(out=bass.AP(tap(c.d[nm]), t0, [[T, 64], [1, TG]]),
                                         in_=ob[:, j, :]))
            st_o = c.LD.v
        barrier(c)

    for seq in range(NS):
        s0 = seq * LSEQ
        with ExitStack() as es:
            c.es = es
            q2 = sb(c, "ma_q", [64, LSEQ], BF16)
            k2 = sb(c, "ma_k", [64, LSEQ], BF16)
            v2 = sb(c, "ma_v", [64, LSEQ], BF16)
            Vt = sb(c, "ma_vt", [128, NTT, 64], BF16)
            P = [sb(c, "ma_P", [128, LSEQ], BF16) for _ in range(2)]
            PT = sb(c, "ma_PT", [128, NTT, 128], BF16)
            fl = sb(c, "ma_fl", [128, 4], F32)
            ob = sb(c, "ma_ob", [64, 2, 128], BF16)
            psS = pst(c, "ma_ps", [128, LSEQ], F32)
            psT = pst(c, "ma_pt", [128, 1024], BF16)
            psO = pst(c, "ma_po", [64, 128], F32)

            @c.block.sync
            def _(e):
                for src, dst in (("q2_sp", q2), ("k2_sp", k2), ("v2_sp", v2)):
                    c.LD.dma(e.dma_start(out=dst[:], in_=bass.AP(tap(c.d[src]), s0,
                                                                 [[T, 64], [1, LSEQ]])))
            ld_q = c.LD.v

            ve_vt = {}
            for g8 in range(2):
                @c.block.tensor
                def _(t):
                    t.wait_ge(c.LD.h, ld_q)
                    if g8:
                        t.wait_ge(c.VE.h, ve_vt[g8 - 1])
                    for b in range(8):
                        tt = g8 * 8 + b
                        t.transpose(psT[:, b * 128:b * 128 + 64],
                                    v2[:, tt * 128:(tt + 1) * 128],
                                    c.id_bf[0:64, 0:64])
                    c.PE.inc(t.last)
                pe_v = c.PE.v

                @c.block.vector
                def _(v):
                    v.wait_ge(c.PE.h, pe_v)
                    dstp = bass.AP(Vt, g8 * 8 * 64, [[NTT * 64, 128], [64, 8], [1, 64]])
                    v.tensor_copy(out=dstp, in_=bass.AP(psT, 0, [[1024, 128], [128, 8], [1, 64]]))
                    c.VE.inc(v.last)
                ve_vt[g8] = c.VE.v
            ve_vtl = c.VE.v

            se_e = {}
            ve_n = {}
            se_pt = {}
            pe_pv = {}
            se_o = {}
            st_o = {}
            prev_e = 0
            prev_o = 0
            prev_pv = 0
            for ii in range(NTT + 1):
                if ii < NTT:
                    i = ii
                    ext = (i + 1) * 128

                    @c.block.tensor
                    def _(t):
                        if prev_e:
                            t.wait_ge(c.SE.h, prev_e)
                        else:
                            t.wait_ge(c.LD.h, ld_q)
                        for j in range((ext + TG - 1) // TG):
                            n = min(TG, ext - j * TG)
                            t.matmul(psS[:, j * TG:j * TG + n],
                                     q2[:, i * 128:(i + 1) * 128],
                                     k2[:, j * TG:j * TG + n],
                                     start=True, stop=True)
                        c.PE.inc(t.last)
                    pe_s = c.PE.v

                    @c.block.vector
                    def _(v):
                        v.wait_ge(c.PE.h, pe_s)
                        v.tensor_tensor(out=psS[:, i * 128:(i + 1) * 128],
                                        in0=psS[:, i * 128:(i + 1) * 128],
                                        in1=c.cm[:], op=ALU.add)
                        v.reduce_max(out=fl[:, 1:2], in_=psS[:, 0:ext], axis=AXX,
                                     negate=True)
                        c.VE.inc(v.last)
                    ve_m = c.VE.v

                    @c.block.scalar
                    def _(s):
                        s.wait_ge(c.VE.h, ve_m)
                        s.activation(out=P[i % 2][:, 0:ext], in_=psS[:, 0:ext],
                                     func=ACT.Exp, bias=fl[:, 1:2], accum_out=fl[:, 2:3])
                        c.SE.inc(s.last)
                    se_e[i] = c.SE.v
                    prev_e = c.SE.v

                    @c.block.vector
                    def _(v):
                        v.wait_ge(c.SE.h, se_e[i])
                        v.reciprocal(out=fl[:, 3:4], in_=fl[:, 2:3])
                        c.VE.inc(v.last)
                    ve_r = c.VE.v

                    @c.block.scalar
                    def _(s):
                        s.wait_ge(c.VE.h, ve_r)
                        s.activation(out=P[i % 2][:, 0:ext], in_=P[i % 2][:, 0:ext],
                                     func=ACT.Identity, scale=fl[:, 3:4])
                        c.SE.inc(s.last)
                    ve_n[i] = c.SE.v

                if ii >= 1:
                    j = ii - 1
                    nb = j + 1
                    for c0 in range(0, nb, 8):
                        nblk = min(8, nb - c0)

                        @c.block.tensor
                        def _(t):
                            if c0 == 0:
                                t.wait_ge(c.SE.h, ve_n[j])
                                if j == 0:
                                    t.wait_ge(c.VE.h, ve_vtl)
                            else:
                                t.wait_ge(c.VE.h, se_pt[(j, c0 - 8)])
                            for b in range(nblk):
                                cc = c0 + b
                                t.transpose(psT[:, b * 128:(b + 1) * 128],
                                            P[j % 2][:, cc * 128:(cc + 1) * 128],
                                            c.id_bf[:])
                            c.PE.inc(t.last)
                        pe_t = c.PE.v

                        @c.block.vector
                        def _(v):
                            v.wait_ge(c.PE.h, pe_t)
                            if prev_pv:
                                v.wait_ge(c.PE.h, prev_pv)
                            dstp = bass.AP(PT, c0 * 128,
                                           [[NTT * 128, 128], [128, nblk], [1, 128]])
                            v.tensor_copy(out=dstp, in_=psT[:, 0:nblk * 128])
                            c.VE.inc(v.last)
                        se_pt[(j, c0)] = c.VE.v
                    last_pt = c.VE.v

                    @c.block.tensor
                    def _(t):
                        t.wait_ge(c.VE.h, last_pt)
                        if prev_o:
                            t.wait_ge(c.SE.h, prev_o)
                        for cc in range(j + 1):
                            t.matmul(psO[:, :], Vt[:, cc, :], PT[:, cc, :],
                                     start=(cc == 0), stop=(cc == j))
                        c.PE.inc(t.last)
                    pe_pv[j] = c.PE.v
                    prev_pv = c.PE.v

                    @c.block.scalar
                    def _(s):
                        s.wait_ge(c.PE.h, pe_pv[j])
                        if st_o.get(j - 2) is not None:
                            s.wait_ge(c.LD.h, st_o[j - 2])
                        s.copy(out=ob[:, j % 2, :], in_=psO[:, :])
                        c.SE.inc(s.last)
                    se_o[j] = c.SE.v
                    prev_o = c.SE.v

                    @c.block.sync
                    def _(e):
                        e.wait_ge(c.SE.h, se_o[j])
                        dap = bass.AP(tap(c.d["ao_sp"]), s0 + j * 128, [[T, 64], [1, 128]])
                        c.LD.dma(e.dma_start(out=dap, in_=ob[:, j % 2, :]))
                    st_o[j] = c.LD.v
            barrier(c)
        c.es = None

    with ExitStack() as es:
        c.es = es
        at = [sb(c, "mo_a", [64, TG], BF16) for _ in range(2)]
        ae = sb(c, "mo_ae", [64, TG], BF16)
        rwb = sb(c, "mo_rw", [64, TG], BF16)
        ope = [sb(c, "mo_w", [64, 1024], BF16) for _ in range(2)]
        ot = [sb(c, "mo_o", [128, TG], F32) for _ in range(2)]
        psA = [pst(c, "mo_ps", [128, TG], F32) for _ in range(8)]

        st_o = {}
        for tg in range(NTG):
            t0 = tg * TG

            @c.block.sync
            def _(e):
                if tg >= 2:
                    e.wait_ge(c.VE.h, ve_ml)
                c.LD.dma(e.dma_start(out=at[tg % 2][:],
                                     in_=bass.AP(tap(c.d["ao_sp"]), t0, [[T, 64], [1, TG]])))
            ld_a = c.LD.v

            for e_i in range(16):
                @c.block.gpsimd
                def _(g):
                    if e_i >= 2:
                        g.wait_ge(c.PE.h, pe_m[e_i - 2])
                    elif tg > 0:
                        g.wait_ge(c.PE.h, pe_ml)
                    src = bass.AP(tap(c.d["l2_op"]), e_i * 64 * 1024,
                                  [[1024, 64], [1, 1024]])
                    g.dma_start(out=ope[e_i % 2][:], in_=src)
                    c.GD.inc(g.last, 16)
                    if True:
                        g.wait_ge(c.VE.h, ve_m.get(e_i - 1, 0) if e_i else
                                  (ve_ml if tg > 0 else 0))
                        g.dma_start(out=rwb[:], in_=row_ap(c.d["rw_sp"], T, e_i, TG,
                                                           off=t0, repl=64))
                        c.GD.inc(g.last, 16)
                gd_w = c.GD.v

                @c.block.vector
                def _(v):
                    v.wait_ge(c.GD.h, gd_w)
                    if e_i == 0:
                        v.wait_ge(c.LD.h, ld_a)
                    if e_i >= 1:
                        v.wait_ge(c.PE.h, pe_m[e_i - 1])
                    v.tensor_tensor(out=ae[:], in0=at[tg % 2][:], in1=rwb[:], op=ALU.mult)
                    c.VE.inc(v.last)
                if e_i == 0:
                    ve_m, pe_m = {}, {}
                ve_m[e_i] = c.VE.v

                @c.block.tensor
                def _(t):
                    t.wait_ge(c.VE.h, ve_m[e_i])
                    t.wait_ge(c.GD.h, gd_w)
                    if e_i == 0 and tg >= 2:
                        for oi in range(8):
                            t.wait_ge(c.SE.h, se_ev[oi])
                    for oi in range(8):
                        t.matmul(psA[oi][:, :], ope[e_i % 2][:, oi * 128:(oi + 1) * 128],
                                 ae[:], start=(e_i == 0), stop=(e_i == 15))
                    c.PE.inc(t.last)
                pe_m[e_i] = c.PE.v
            pe_ml = c.PE.v
            ve_ml = ve_m[15]

            if tg == 0:
                se_ev = {}
            for oi in range(8):
                @c.block.scalar
                def _(s):
                    s.wait_ge(c.PE.h, pe_ml)
                    lk = st_o.get((tg, oi - 2)) or (st_o.get((tg - 1, 6 + oi)) if tg else None)
                    if lk:
                        s.wait_ge(c.LD.h, lk)
                    s.activation(out=ot[oi % 2][:], in_=psA[oi][:, :], func=ACT.Silu)
                    c.SE.inc(s.last)
                se_ev[oi] = c.SE.v

                @c.block.sync
                def _(e):
                    e.wait_ge(c.SE.h, se_ev[oi])
                    dap = bass.AP(tap(c.d["g_sp"]), oi * 128 * T + t0, [[T, 128], [1, TG]])
                    c.LD.dma(e.dma_start(out=dap, in_=ot[oi % 2][:]))
                st_o[(tg, oi)] = c.LD.v
        barrier(c)



def phase_l3_linattn(c):
    phase_proj(c, c.d["xn_sp"], c.d["l3_wq_t"], c.d["q_sp"], D, D,
               bias=c.d["l3_bq"], act="elu1", out_dt=BF16)
    phase_proj(c, c.d["xn_sp"], c.d["l3_wk_t"], c.d["k_sp"], D, D,
               bias=c.d["l3_bk"], act="elu1", out_dt=BF16)

    for seq in range(NS):
        s0 = seq * LSEQ
        with ExitStack() as esA:
            c.es = esA
            Qf = sb(c, "la_Q", [64, 16, LSEQ], BF16)
            Kt = sb(c, "la_Kt", [128, NTT, D], BF16)
            Vt = sb(c, "la_V", [128, NTT, D], BF16)
            bvb = sb(c, "la_bvb", [128, D], F32)
            ksr = sb(c, "la_ks", [1, D], F32)
            ksc = sb(c, "la_kc", [64, 16], BF16)
            kvh = sb(c, "la_kv", [64, 16, 64], BF16)
            zr = sb(c, "la_z", [1, LSEQ], F32)
            zb = sb(c, "la_zb", [64, LSEQ], F32)
            ob = sb(c, "la_ob", [64, LSEQ], BF16)

            @c.block.sync
            def _(e):
                qap = bass.AP(tap(c.d["q_sp"]), s0, [[T, 64], [64 * T, 16], [1, LSEQ]])
                c.LD.dma(e.dma_start(out=Qf[:], in_=qap))
            ld_qk = c.LD.v

            with ExitStack() as esB:
                c.es = esB
                wv_sl = sb(c, "la_wv", [128, DC, D], BF16)
                xnt = [sb(c, "la_xnt", [128, DC, 128], BF16) for _ in range(2)]
                psV = [pst(c, "la_psV", [128, D], F32) for _ in range(2)]
                c.es = esA

                @c.block.gpsimd
                def _(g):
                    src = bass.AP(tap(c.d["l3_wv_t"]), 0,
                                  [[D, 128], [128 * D, DC], [1, D]])
                    c.GD.dma(g.dma_start(out=wv_sl[:], in_=src))
                    c.GD.dma(g.dma_start(out=bvb[:],
                                         in_=bass.AP(tap(c.d["l3_bv"]), 0,
                                                     [[0, 128], [1, D]])))
                gd_wv = c.GD.v

                pe_v = {}
                ve_v = {}
                for tt in range(NTT):
                    @c.block.sync
                    def _(e):
                        if pe_v.get(tt - 2):
                            e.wait_ge(c.PE.h, pe_v[tt - 2])
                        c.LD.dma(e.dma_start(out=xnt[tt % 2][:],
                                             in_=fm_ap(c.d["xn_sp"], s0 + tt * 128, 128)))
                    ld_x = c.LD.v

                    @c.block.tensor
                    def _(t):
                        t.wait_ge(c.LD.h, ld_x)
                        if tt == 0:
                            t.wait_ge(c.GD.h, gd_wv)
                        if ve_v.get(tt - 2):
                            t.wait_ge(c.VE.h, ve_v[tt - 2])
                        for vh in range(2):
                            for dc in range(DC):
                                t.matmul(psV[tt % 2][:, vh * 512:(vh + 1) * 512],
                                         xnt[tt % 2][:, dc, :],
                                         wv_sl[:, dc, vh * 512:(vh + 1) * 512],
                                         start=(dc == 0), stop=(dc == DC - 1))
                        c.PE.inc(t.last)
                    pe_v[tt] = c.PE.v

                    @c.block.vector
                    def _(v):
                        v.wait_ge(c.PE.h, pe_v[tt])
                        v.tensor_tensor(out=Vt[:, tt, :], in0=psV[tt % 2][:, :],
                                        in1=bvb[:], op=ALU.add)
                        c.VE.inc(v.last)
                    ve_v[tt] = c.VE.v
                barrier(c)
                c.es = esA

            psT = pst(c, "la_pt", [128, 1024], BF16)
            psKV = pst(c, "la_pkv", [64, 64], F32)
            psZ = pst(c, "la_pz", [1, LSEQ], F32)
            psC = pst(c, "la_pc", [64, TG], F32)

            with ExitStack() as esK:
                c.es = esK
                Kf = sb(c, "la_K", [128, DC, LSEQ], BF16)
                c.es = esA

                @c.block.sync
                def _(e):
                    c.LD.dma(e.dma_start(out=Kf[:], in_=fm_ap(c.d["k_sp"], s0, LSEQ)))
                ld_kf = c.LD.v

                ve_kt = 0
                for tt in range(NTT):
                    @c.block.tensor
                    def _(t):
                        if tt == 0:
                            t.wait_ge(c.LD.h, ld_kf)
                        if ve_kt:
                            t.wait_ge(c.VE.h, ve_kt)
                        for dc in range(DC):
                            t.transpose(psT[:, dc * 128:(dc + 1) * 128],
                                        Kf[:, dc, tt * 128:(tt + 1) * 128], c.id_bf[:])
                        c.PE.inc(t.last)
                    pe_kt = c.PE.v

                    @c.block.vector
                    def _(v):
                        v.wait_ge(c.PE.h, pe_kt)
                        v.tensor_copy(out=Kt[:, tt, :], in_=psT[:, :])
                        c.VE.inc(v.last)
                    ve_kt = c.VE.v
                barrier(c)
                c.es = esA

            @c.block.tensor
            def _(t):
                t.wait_ge(c.VE.h, ve_kt)
                for half in range(2):
                    for tt in range(NTT):
                        t.matmul(psZ[:, half * 512:half * 512 + 512],
                                 c.ones_bf[:], Kt[:, tt, half * 512:half * 512 + 512],
                                 start=(tt == 0), stop=(tt == NTT - 1))
                c.PE.inc(t.last)
            pe_ks = c.PE.v

            @c.block.vector
            def _(v):
                v.wait_ge(c.PE.h, pe_ks)
                v.tensor_scalar(out=ksr[:], in0=psZ[:, 0:1024], scalar1=1e-6,
                                scalar2=None, op0=ALU.add)
                c.VE.inc(v.last)
            ve_ks = c.VE.v

            @c.block.sync
            def _(e):
                e.wait_ge(c.VE.h, ve_ks)
                c.LD.dma(e.dma_start(out=c.d["bc2_scr"][1:2, 0:1024], in_=ksr[:]))
                e.wait_ge(c.LD.h, c.LD.v)

            @c.block.gpsimd
            def _(g):
                g.wait_ge(c.LD.h, c.LD.v)
                g.dma_start(out=ksc[:], in_=bass.AP(tap(c.d["bc2_scr"]), 2048,
                                                    [[1, 64], [64, 16]]),
                            allow_slow_non_contiguous=True)
                c.GD.inc(g.last, 16)
            gd_kc = c.GD.v

            prev_ve_ctx = 0
            prev_se_kv = 0
            st_o = 0
            for h in range(16):
                @c.block.tensor
                def _(t):
                    if h == 0:
                        t.wait_ge(c.GD.h, gd_kc)
                        t.wait_ge(c.LD.h, ld_qk)
                    if prev_se_kv:
                        t.wait_ge(c.SE.h, prev_se_kv)
                    for tt in range(NTT):
                        t.matmul(psKV[:, :], Kt[:, tt, h * 64:(h + 1) * 64],
                                 Vt[:, tt, h * 64:(h + 1) * 64],
                                 start=(tt == 0), stop=(tt == NTT - 1))
                    if prev_ve_ctx:
                        t.wait_ge(c.VE.h, prev_ve_ctx)
                    for ch in range(4):
                        t.matmul(psZ[:, ch * 512:(ch + 1) * 512],
                                 ksc[:, h:h + 1],
                                 Qf[:, h, ch * 512:(ch + 1) * 512],
                                 start=True, stop=True)
                    c.PE.inc(t.last)
                pe_kv = c.PE.v

                @c.block.scalar
                def _(s):
                    s.wait_ge(c.PE.h, pe_kv)
                    s.copy(out=kvh[:, h, :], in_=psKV[:, :])
                    c.SE.inc(s.last)
                se_kv = c.SE.v
                prev_se_kv = se_kv

                @c.block.vector
                def _(v):
                    v.wait_ge(c.PE.h, pe_kv)
                    v.reciprocal(out=zr[:], in_=psZ[:, :])
                    c.VE.inc(v.last)
                ve_z = c.VE.v

                @c.block.sync
                def _(e):
                    e.wait_ge(c.VE.h, ve_z)
                    c.LD.dma(e.dma_start(out=c.d["bc2_scr"][2:3, :], in_=zr[:]))
                    e.wait_ge(c.LD.h, c.LD.v)
                    c.LD.dma(e.dma_start(out=zb[:], in_=row_ap(c.d["bc2_scr"], 2048,
                                                               2, LSEQ, repl=64)))
                ld_zb = c.LD.v

                @c.block.tensor
                def _(t):
                    t.wait_ge(c.SE.h, se_kv)
                    t.matmul(psC[:, 0:512], kvh[:, h, :], Qf[:, h, 0:512],
                             start=True, stop=True)
                    c.PE.inc(t.last)
                pe_c0 = c.PE.v

                ve_cx = 0
                for ch in range(4):
                    if ch > 0:
                        @c.block.tensor
                        def _(t):
                            t.wait_ge(c.VE.h, ve_cx)
                            t.matmul(psC[:, 0:512], kvh[:, h, :],
                                     Qf[:, h, ch * 512:(ch + 1) * 512],
                                     start=True, stop=True)
                            c.PE.inc(t.last)
                        pe_cc = c.PE.v
                    else:
                        pe_cc = pe_c0

                    @c.block.vector
                    def _(v):
                        v.wait_ge(c.PE.h, pe_cc)
                        if ch == 0:
                            v.wait_ge(c.LD.h, ld_zb)
                            if st_o:
                                v.wait_ge(c.LD.h, st_o)
                        v.tensor_tensor(out=ob[:, ch * 512:(ch + 1) * 512],
                                        in0=psC[:, 0:512],
                                        in1=zb[:, ch * 512:(ch + 1) * 512], op=ALU.mult)
                        c.VE.inc(v.last)
                    ve_cx = c.VE.v
                prev_ve_ctx = ve_cx

                @c.block.sync
                def _(e):
                    e.wait_ge(c.VE.h, ve_cx)
                    if st_o:
                        e.wait_ge(c.LD.h, st_o)
                    dap = bass.AP(tap(c.d["o_att"]), (h * 64) * T + s0,
                                  [[T, 64], [1, LSEQ]])
                    c.LD.dma(e.dma_start(out=dap, in_=ob[:]))
                st_o = c.LD.v
            barrier(c)
        c.es = None

    phase_proj(c, c.d["o_att"], c.d["l3_wd_t"], c.d["gacc"], D, D,
               bias=c.d["l3_bd"], act="copy", out_dt=F32)

    with ExitStack() as es:
        c.es = es
        gam = sb(c, "ln3_g", [128, DC], F32)
        bet = sb(c, "ln3_b", [128, DC], F32)
        ht = sb(c, "ln3_h", [128, DC, TG], F32)
        xnt = sb(c, "ln3_x", [128, DC, TG], F32)
        hsq = sb(c, "ln3_q", [128, DC, TG], F32)
        ab = sb(c, "ln3_ab", [1, 8, TG], F32)
        abb = sb(c, "ln3_bb", [128, 2, TG], F32)
        ot = sb(c, "ln3_o", [128, DC, TG], F32)
        stats = pst(c, "ln3_st", [64, TG], F32)

        @c.block.sync
        def _(e):
            c.LD.dma(e.dma_start(out=gam[:], in_=c.d["l3_gam"]))
            c.LD.dma(e.dma_start(out=bet[:], in_=c.d["l3_bet"]))
        ld_gb = c.LD.v

        prev_pe = 0
        prev_st = 0
        for tg in range(NTG):
            t0 = tg * TG

            @c.block.sync
            def _(e):
                if prev_pe:
                    e.wait_ge(c.PE.h, prev_pe)
                c.LD.dma(e.dma_start(out=ht[:], in_=fm_ap(c.d["gacc"], t0, TG)))

            @c.block.gpsimd
            def _(g):
                if prev_pe:
                    g.wait_ge(c.PE.h, prev_pe)
                g.dma_start(out=xnt[:], in_=fm_ap(c.d["xn_sp"], t0, TG))
                c.GD.inc(g.last, 16)
            ld_h = c.LD.v
            gd_x = c.GD.v

            @c.block.vector
            def _(v):
                v.wait_ge(c.LD.h, ld_h)
                v.wait_ge(c.GD.h, gd_x)
                if prev_st:
                    v.wait_ge(c.LD.h, prev_st)
                v.tensor_tensor(out=ht[:], in0=ht[:], in1=xnt[:], op=ALU.add)
                v.tensor_tensor(out=hsq[:], in0=ht[:], in1=ht[:], op=ALU.mult)
                c.VE.inc(v.last)
            ve_h = c.VE.v

            @c.block.tensor
            def _(t):
                t.wait_ge(c.VE.h, ve_h)
                for dc in range(DC):
                    t.matmul(stats[0:1, :], c.ones_f[:], ht[:, dc, :],
                             start=(dc == 0), stop=(dc == DC - 1))
                for dc in range(DC):
                    t.matmul(stats[32:33, :], c.ones_f[:], hsq[:, dc, :],
                             start=(dc == 0), stop=(dc == DC - 1))
                c.PE.inc(t.last)
            prev_pe = c.PE.v

            @c.block.vector
            def _(e):
                e.wait_ge(c.PE.h, prev_pe)
                e.tensor_scalar(out=ab[0:1, 0, :], in0=stats[0:1, :], scalar1=1.0 / D,
                                scalar2=None, op0=ALU.mult)
                e.tensor_scalar(out=ab[0:1, 1, :], in0=stats[32:33, :], scalar1=1.0 / D,
                                scalar2=None, op0=ALU.mult)
                e.tensor_tensor(out=ab[0:1, 2, :], in0=ab[0:1, 0, :], in1=ab[0:1, 0, :], op=ALU.mult)
                e.tensor_tensor(out=ab[0:1, 2, :], in0=ab[0:1, 1, :], in1=ab[0:1, 2, :], op=ALU.subtract)
                e.tensor_scalar(out=ab[0:1, 2, :], in0=ab[0:1, 2, :], scalar1=1e-12,
                                scalar2=None, op0=ALU.add)
                e.reciprocal(out=ab[0:1, 2, :], in_=ab[0:1, 2, :])
                c.VE.inc(e.last)
            ve_var = c.VE.v

            @c.block.scalar
            def _(e):
                e.wait_ge(c.VE.h, ve_var)
                e.activation(out=ab[0:1, 3, :], in_=ab[0:1, 2, :], func=ACT.Sqrt)
                c.SE.inc(e.last)
            se_sq = c.SE.v

            @c.block.vector
            def _(e):
                e.wait_ge(c.SE.h, se_sq)
                e.tensor_tensor(out=ab[0:1, 4, :], in0=ab[0:1, 0, :], in1=ab[0:1, 3, :], op=ALU.mult)
                e.tensor_scalar(out=ab[0:1, 5, :], in0=ab[0:1, 4, :], scalar1=0.0,
                                scalar2=None, op0=ALU.mult)
                e.tensor_tensor(out=ab[0:1, 4, :], in0=ab[0:1, 5, :], in1=ab[0:1, 4, :], op=ALU.subtract)
                c.VE.inc(e.last)
            ve_b = c.VE.v

            @c.block.gpsimd
            def _(e):
                e.wait_ge(c.VE.h, ve_b)
                c.GD.dma(e.dma_start(out=c.d["bc_scr"][6:8, :], in_=ab[0:1, 3:5, :]))
                e.wait_ge(c.GD.h, c.GD.v)
                c.GD.dma(e.dma_start(out=abb[:, 0, :], in_=row_ap(c.d["bc_scr"], TG, 6, TG)))
                c.GD.dma(e.dma_start(out=abb[:, 1, :], in_=row_ap(c.d["bc_scr"], TG, 7, TG)))
            gd_ab = c.GD.v

            se_ap = {}
            for dc in range(DC):
                @c.block.vector
                def _(v):
                    if dc == 0:
                        v.wait_ge(c.GD.h, gd_ab)
                        if tg == 0:
                            v.wait_ge(c.LD.h, ld_gb)
                    a_b = bass.AP(abb, 0, [[2 * TG, 128], [1, TG]])
                    b_b = bass.AP(abb, TG, [[2 * TG, 128], [1, TG]])
                    v.tensor_tensor(out=hsq[:, dc, :], in0=ht[:, dc, :], in1=a_b, op=ALU.mult)
                    v.tensor_tensor(out=hsq[:, dc, :], in0=hsq[:, dc, :], in1=b_b, op=ALU.add)
                    c.VE.inc(v.last)
                ve_ap = c.VE.v

                @c.block.scalar
                def _(s):
                    s.wait_ge(c.VE.h, ve_ap)
                    s.activation(out=ot[:, dc, :], in_=hsq[:, dc, :], func=ACT.Identity,
                                 scale=gam[:, dc:dc + 1], bias=bet[:, dc:dc + 1])
                    c.SE.inc(s.last)
                se_ap[dc] = c.SE.v

            @c.block.sync
            def _(e):
                e.wait_ge(c.SE.h, se_ap[DC - 1])
                c.LD.dma(e.dma_start(out=fm_ap(c.d["s_sp"], t0, TG), in_=ot[:]))
            prev_st = c.LD.v
        barrier(c)



def emit_fp4_ve(c, v, xf, aw, sg, q, m, sc3, NB):
    v.reduce_max(out=sc3[:, :, 0:1], in_=bass.AP(aw, 0, [[NB * 16, 128], [16, NB], [1, 16]]),
                 axis=AXX)
    v.tensor_scalar(out=sc3[:, :, 0:1], in0=sc3[:, :, 0:1], scalar1=1.0 / 6.0,
                    scalar2=1e-12, op0=ALU.mult, op1=ALU.max)
    v.reciprocal(out=sc3[:, :, 1:2], in_=sc3[:, :, 0:1])
    rs_b = bass.AP(sc3, 1, [[2 * NB, 128], [2, NB], [0, 16]])
    sc_b = bass.AP(sc3, 0, [[2 * NB, 128], [2, NB], [0, 16]])
    v.tensor_tensor(out=aw[:], in0=aw[:], in1=rs_b, op=ALU.mult)
    v.tensor_scalar(out=q[:], in0=aw[:], scalar1=0.25, scalar2=None, op0=ALU.is_gt)
    for thr in (0.75, 1.25, 1.75):
        v.tensor_scalar(out=m[:], in0=aw[:], scalar1=thr, scalar2=None, op0=ALU.is_gt)
        v.tensor_tensor(out=q[:], in0=q[:], in1=m[:], op=ALU.add)
    v.tensor_scalar(out=m[:], in0=aw[:], scalar1=2.5, scalar2=None, op0=ALU.is_gt)
    v.scalar_tensor_tensor(out=q[:], in0=q[:], scalar=0.5, in1=m[:],
                           op0=ALU.mult, op1=ALU.add)
    v.tensor_scalar(out=m[:], in0=aw[:], scalar1=3.5, scalar2=None, op0=ALU.is_gt)
    v.tensor_tensor(out=q[:], in0=q[:], in1=m[:], op=ALU.add)
    v.tensor_scalar(out=m[:], in0=aw[:], scalar1=5.0, scalar2=None, op0=ALU.is_gt)
    v.tensor_tensor(out=m[:], in0=m[:], in1=m[:], op=ALU.add)
    v.tensor_tensor(out=q[:], in0=q[:], in1=m[:], op=ALU.add)
    v.tensor_tensor(out=q[:], in0=q[:], in1=sc_b, op=ALU.mult)
    v.tensor_tensor(out=q[:], in0=q[:], in1=sg[:], op=ALU.mult)


def emit_fp4_quant_t(c, w_dram, out_dram, R, CF):
    NB = CF // 16
    with ExitStack() as es:
        c.es = es
        wt = sb(c, "fq_w", [128, CF], F32)
        aw = sb(c, "fq_a", [128, CF], F32)
        sg = sb(c, "fq_s", [128, CF], F32)
        q = sb(c, "fq_q", [128, CF], F32)
        m = sb(c, "fq_m", [128, CF], F32)
        qb = sb(c, "fq_qb", [128, CF], BF16)
        sc3 = sb(c, "fq_sc", [128, NB, 2], F32)
        tp = sb(c, "fq_tp", [128, 8, 128], BF16)
        psT = pst(c, "fq_pt", [128, 1024], BF16)

        st_q = 0
        pe_pv = 0
        for rc in range(R // 128):
            @c.block.sync
            def _(e):
                if st_q:
                    e.wait_ge(c.LD.h, st_q)
                c.LD.dma(e.dma_start(out=wt[:],
                                     in_=bass.AP(tap(w_dram), rc * 128 * CF,
                                                 [[CF, 128], [1, CF]])))
            ld_w = c.LD.v

            @c.block.scalar
            def _(s):
                s.wait_ge(c.LD.h, ld_w)
                s.activation(out=aw[:], in_=wt[:], func=ACT.Abs)
                s.sign(out=sg[:], in_=wt[:])
                c.SE.inc(s.last)
            se_q = c.SE.v

            @c.block.vector
            def _(v):
                v.wait_ge(c.SE.h, se_q)
                if pe_pv:
                    v.wait_ge(c.PE.h, pe_pv)
                emit_fp4_ve(c, v, wt, aw, sg, q, m, sc3, NB)
                v.tensor_copy(out=qb[:], in_=q[:])
                c.VE.inc(v.last)
            ve_q = c.VE.v

            for c0 in range(0, CF // 128, 8):
                nblk = min(8, CF // 128 - c0)

                @c.block.tensor
                def _(t):
                    if c0 == 0:
                        t.wait_ge(c.VE.h, ve_q)
                    else:
                        t.wait_ge(c.VE.h, ve_tp)
                    for b in range(nblk):
                        cb = c0 + b
                        t.transpose(psT[:, b * 128:(b + 1) * 128],
                                    qb[:, cb * 128:(cb + 1) * 128], c.id_bf[:])
                    c.PE.inc(t.last)
                pe_t = c.PE.v
                pe_pv = pe_t

                @c.block.vector
                def _(v):
                    v.wait_ge(c.PE.h, pe_t)
                    if st_q:
                        v.wait_ge(c.LD.h, st_q)
                    v.tensor_copy(out=tp[:, 0:nblk, :], in_=psT[:, 0:nblk * 128])
                    c.VE.inc(v.last)
                ve_tp = c.VE.v

                @c.block.sync
                def _(e):
                    e.wait_ge(c.VE.h, ve_tp)
                    for b in range(nblk):
                        cb = c0 + b
                        dap = bass.AP(tap(out_dram), (cb * 128) * R + rc * 128,
                                      [[R, 128], [1, 128]])
                        c.LD.dma(e.dma_start(out=dap, in_=tp[:, b, :]))
                st_q = c.LD.v
        barrier(c)


def phase_l3_fp4(c):
    emit_fp4_quant_t(c, c.d["l3_wg"], c.d["wgq_sp"], 2816, 1024)
    emit_fp4_quant_t(c, c.d["l3_wu"], c.d["wuq_sp"], 2816, 1024)
    emit_fp4_quant_t(c, c.d["l3_wdf"], c.d["wdq_sp"], 1024, 2816)

    with ExitStack() as es:
        c.es = es
        xnt = sb(c, "xq_x", [128, DC, 128], BF16)
        xtm = sb(c, "xq_t", [128, DC, 128], BF16)
        xf = sb(c, "xq_f", [128, 1024], F32)
        aw = sb(c, "xq_a", [128, 1024], F32)
        sg = sb(c, "xq_s", [128, 1024], F32)
        q = sb(c, "xq_q", [128, 1024], F32)
        m = sb(c, "xq_m", [128, 1024], F32)
        qb = sb(c, "xq_qb", [128, 1024], BF16)
        sc3 = sb(c, "xq_sc", [128, 64, 2], F32)
        xqf = sb(c, "xq_o", [128, DC, 128], BF16)
        psT = pst(c, "xq_pt", [128, 1024], BF16)

        st_q = 0
        ve_prev = 0
        for tt in range(T // 128):
            @c.block.sync
            def _(e):
                if ve_prev:
                    e.wait_ge(c.VE.h, ve_prev)
                c.LD.dma(e.dma_start(out=xnt[:], in_=fm_ap(c.d["xn_sp"], tt * 128, 128)))
            ld_x = c.LD.v

            @c.block.tensor
            def _(t):
                t.wait_ge(c.LD.h, ld_x)
                if ve_prev:
                    t.wait_ge(c.VE.h, ve_prev)
                for dc in range(DC):
                    t.transpose(psT[:, dc * 128:(dc + 1) * 128],
                                xnt[:, dc, :], c.id_bf[:])
                c.PE.inc(t.last)
            pe_t1 = c.PE.v

            @c.block.vector
            def _(v):
                v.wait_ge(c.PE.h, pe_t1)
                v.tensor_copy(out=xtm[:], in_=psT[:, :])
                v.tensor_copy(out=xf[:], in_=bass.AP(xtm, 0, [[1024, 128], [1, 1024]]))
                c.VE.inc(v.last)
            ve_c = c.VE.v

            @c.block.scalar
            def _(s):
                s.wait_ge(c.VE.h, ve_c)
                s.activation(out=aw[:], in_=xf[:], func=ACT.Abs)
                s.sign(out=sg[:], in_=xf[:])
                c.SE.inc(s.last)
            se_a = c.SE.v

            @c.block.vector
            def _(v):
                v.wait_ge(c.SE.h, se_a)
                emit_fp4_ve(c, v, xf, aw, sg, q, m, sc3, 64)
                v.tensor_copy(out=qb[:], in_=q[:])
                c.VE.inc(v.last)
            ve_q = c.VE.v

            @c.block.tensor
            def _(t):
                t.wait_ge(c.VE.h, ve_q)
                for dc in range(DC):
                    t.transpose(psT[:, dc * 128:(dc + 1) * 128],
                                qb[:, dc * 128:(dc + 1) * 128], c.id_bf[:])
                c.PE.inc(t.last)
            pe_t2 = c.PE.v

            @c.block.vector
            def _(v):
                v.wait_ge(c.PE.h, pe_t2)
                if st_q:
                    v.wait_ge(c.LD.h, st_q)
                v.tensor_copy(out=xqf[:], in_=bass.AP(psT, 0, [[1024, 128], [128, DC], [1, 128]]))
                c.VE.inc(v.last)
            ve_prev = c.VE.v

            @c.block.sync
            def _(e):
                e.wait_ge(c.VE.h, ve_prev)
                c.LD.dma(e.dma_start(out=fm_ap(c.d["xq_sp"], tt * 128, 128), in_=xqf[:]))
            st_q = c.LD.v
        barrier(c)

    phase_proj(c, c.d["xq_sp"], c.d["wgq_sp"], c.d["gate_sp"], D, 2816,
               act="copy", out_dt=F32)
    phase_proj(c, c.d["xq_sp"], c.d["wuq_sp"], c.d["up_sp"], D, 2816,
               act="copy", out_dt=F32)

    with ExitStack() as es:
        c.es = es
        gt = [sb(c, "hg_g", [128, TG], F32) for _ in range(2)]
        ut = [sb(c, "hg_u", [128, TG], F32) for _ in range(2)]
        hb = [sb(c, "hg_h", [128, TG], BF16) for _ in range(2)]
        st_h = {}
        ve_h = {}
        i = 0
        for jc in range(22):
            for tg in range(NTG):
                t0 = tg * TG

                @c.block.sync
                def _(e):
                    if ve_h.get(i - 2) is not None:
                        e.wait_ge(c.VE.h, ve_h[i - 2])
                    c.LD.dma(e.dma_start(out=gt[i % 2][:],
                                         in_=bass.AP(tap(c.d["gate_sp"]), (jc * 128) * T + t0,
                                                     [[T, 128], [1, TG]])))
                    c.LD.dma(e.dma_start(out=ut[i % 2][:],
                                         in_=bass.AP(tap(c.d["up_sp"]), (jc * 128) * T + t0,
                                                     [[T, 128], [1, TG]])))
                ld_g = c.LD.v

                @c.block.scalar
                def _(s):
                    s.wait_ge(c.LD.h, ld_g)
                    s.activation(out=gt[i % 2][:], in_=gt[i % 2][:], func=ACT.Silu)
                    c.SE.inc(s.last)
                se_s = c.SE.v

                @c.block.vector
                def _(v):
                    v.wait_ge(c.SE.h, se_s)
                    if st_h.get(i - 2) is not None:
                        v.wait_ge(c.LD.h, st_h[i - 2])
                    v.tensor_tensor(out=hb[i % 2][:], in0=gt[i % 2][:], in1=ut[i % 2][:],
                                    op=ALU.mult)
                    c.VE.inc(v.last)
                ve_h[i] = c.VE.v

                @c.block.sync
                def _(e):
                    e.wait_ge(c.VE.h, ve_h[i])
                    c.LD.dma(e.dma_start(out=bass.AP(tap(c.d["h2_sp"]), (jc * 128) * T + t0,
                                                     [[T, 128], [1, TG]]), in_=hb[i % 2][:]))
                st_h[i] = c.LD.v
                i += 1
        barrier(c)

    with ExitStack() as es:
        c.es = es
        hbt = sb(c, "hq_x", [128, 22, 128], BF16)
        htm = sb(c, "hq_t", [128, 22, 128], BF16)
        xf = sb(c, "hq_f", [128, 2816], F32)
        aw = sb(c, "hq_a", [128, 2816], F32)
        sg = sb(c, "hq_s", [128, 2816], F32)
        q = sb(c, "hq_q", [128, 2816], F32)
        m = sb(c, "hq_m", [128, 2816], F32)
        qb = sb(c, "hq_qb", [128, 2816], BF16)
        sc3 = sb(c, "hq_sc", [128, 176, 2], F32)
        hqf = sb(c, "hq_o", [128, 22, 128], BF16)
        psT = pst(c, "hq_pt", [128, 1024], BF16)

        st_q = 0
        ve_prev = 0
        for tt in range(T // 128):
            @c.block.sync
            def _(e):
                if ve_prev:
                    e.wait_ge(c.VE.h, ve_prev)
                c.LD.dma(e.dma_start(out=hbt[:],
                                     in_=fm_ap(c.d["h2_sp"], tt * 128, 128, nchunk=22)))
            ld_x = c.LD.v

            ve_tc = 0
            for c0 in range(0, 22, 8):
                nblk = min(8, 22 - c0)

                @c.block.tensor
                def _(t):
                    if c0 == 0:
                        t.wait_ge(c.LD.h, ld_x)
                        if ve_prev:
                            t.wait_ge(c.VE.h, ve_prev)
                    else:
                        t.wait_ge(c.VE.h, ve_tc)
                    for b in range(nblk):
                        t.transpose(psT[:, b * 128:(b + 1) * 128],
                                    hbt[:, c0 + b, :], c.id_bf[:])
                    c.PE.inc(t.last)
                pe_t = c.PE.v

                @c.block.vector
                def _(v):
                    v.wait_ge(c.PE.h, pe_t)
                    v.tensor_copy(out=htm[:, c0:c0 + nblk, :], in_=psT[:, 0:nblk * 128])
                    c.VE.inc(v.last)
                ve_tc = c.VE.v

            @c.block.vector
            def _(v):
                v.tensor_copy(out=xf[:], in_=bass.AP(htm, 0, [[22 * 128, 128], [1, 2816]]))
                c.VE.inc(v.last)
            ve_c = c.VE.v

            @c.block.scalar
            def _(s):
                s.wait_ge(c.VE.h, ve_c)
                s.activation(out=aw[:], in_=xf[:], func=ACT.Abs)
                s.sign(out=sg[:], in_=xf[:])
                c.SE.inc(s.last)
            se_a = c.SE.v

            @c.block.vector
            def _(v):
                v.wait_ge(c.SE.h, se_a)
                emit_fp4_ve(c, v, xf, aw, sg, q, m, sc3, 176)
                v.tensor_copy(out=qb[:], in_=q[:])
                c.VE.inc(v.last)
            ve_q = c.VE.v

            ve_bk = 0
            for c0 in range(0, 22, 8):
                nblk = min(8, 22 - c0)

                @c.block.tensor
                def _(t):
                    if c0 == 0:
                        t.wait_ge(c.VE.h, ve_q)
                    else:
                        t.wait_ge(c.VE.h, ve_bk)
                    for b in range(nblk):
                        t.transpose(psT[:, b * 128:(b + 1) * 128],
                                    qb[:, (c0 + b) * 128:(c0 + b + 1) * 128], c.id_bf[:])
                    c.PE.inc(t.last)
                pe_t = c.PE.v

                @c.block.vector
                def _(v):
                    v.wait_ge(c.PE.h, pe_t)
                    if st_q and c0 == 0:
                        v.wait_ge(c.LD.h, st_q)
                    v.tensor_copy(out=hqf[:, c0:c0 + nblk, :], in_=psT[:, 0:nblk * 128])
                    c.VE.inc(v.last)
                ve_bk = c.VE.v
            ve_prev = ve_bk

            @c.block.sync
            def _(e):
                e.wait_ge(c.VE.h, ve_prev)
                c.LD.dma(e.dma_start(out=fm_ap(c.d["hq_sp"], tt * 128, 128, nchunk=22),
                                     in_=hqf[:]))
            st_q = c.LD.v
        barrier(c)

    phase_proj(c, c.d["hq_sp"], c.d["wdq_sp"], c.d["g_sp"], 2816, D,
               act="silu", out_dt=F32)



def build_nc(n_layers):
    nc = bass.Bass("TRN2", target_bir_lowering=False, detect_race_conditions=False)
    c = C()
    c.nc = nc
    c.uid = 0
    c.bar_n = 0
    d = {}
    c.d = d

    def din(name, shape, dt=F32):
        d[name] = nc.dram_tensor(name, shape, dt, kind="ExternalInput").ap()

    def dscr(name, shape, dt=F32):
        kind = "ExternalOutput" if (DEBUG_TAPS and name in
                                    ("g_sp", "wfm_dram", "xn_sp", "s_sp")) else "Internal"
        d[name] = nc.dram_tensor(name, shape, dt, kind=kind).ap()

    din("x_fm", [D, T])
    d["out_fm"] = nc.dram_tensor("out_fm", [D, T], F32, kind="ExternalOutput").ap()

    din("ident_bf", [128, 128], BF16)
    din("ident_f", [128, 128], F32)
    din("cmask", [128, 128], F32)
    din("ones_col", [128, 1], F32)
    din("consts", [128, 2], F32)
    din("ones_bfc", [128, 1], BF16)

    din("wq0", [D, D]); din("wk0", [D, D]); din("wv0", [D, D]); din("wo0", [D, D])
    din("w1", [8, D, 4096]); din("b1_fm", [128, 256])
    din("w2", [8, 4096, D]); din("b2", [8, D])
    din("wr", [D, 8]); din("br_col", [8, 1])

    if n_layers >= 2:
        for nm in ("wq1", "wk1", "wv1", "wo1"):
            din(nm, [D, D])
        for nm in ("bq1", "bk1", "bo1"):
            din(nm, [128, DC])
        din("bv1", [D])
        din("l1_wfc_t", [1024, 4096]); din("l1_wpr_t", [4096, 1024])
        din("l1_lnw", [128, 32]); din("l1_lnb", [128, 32])
        dscr("wq1_sp", [1024, 4096], BF16)
        dscr("wq2_sp", [4096, 1024], BF16)
        dscr("h_sp", [4096, T])

    if n_layers >= 3:
        din("l2_ipw_t", [1024, 4096])
        din("l2_convw", [128, 16, 4]); din("l2_convb", [128, 16])
        din("l2_xpw_t", [2048, 96])
        din("l2_dthw_t", [64, 2048]); din("l2_dthb", [128, 16])
        din("l2_opw_t", [2048, 1024]); din("l2_D", [128, 16])
        din("l2_sim", [1024, 16]); din("l2_gates", [16])
        din("l2_qp", [16, 1024, 64]); din("l2_kp", [16, 1024, 64])
        din("l2_vp", [16, 1024, 64]); din("l2_op", [16, 64, 1024])
        dscr("xc_sp", [2048, T])
        dscr("xcb_sp", [2048, T], BF16)
        dscr("xdbl_sp", [96, T])
        dscr("kks_sp", [1, T])
        dscr("dtv_sp", [2048, T])
        dscr("xdt_sp", [2048, T], BF16)
        dscr("yz_sp", [2048, T], BF16)
        dscr("rw_sp", [16, T])
        dscr("q2_sp", [64, T], BF16)
        dscr("k2_sp", [64, T], BF16)
        dscr("v2_sp", [64, T], BF16)
        dscr("ao_sp", [64, T], BF16)
        dscr("bc2_scr", [4, 2048])

    if n_layers >= 4:
        for nm in ("l3_wq_t", "l3_wk_t", "l3_wv_t", "l3_wd_t"):
            din(nm, [D, D])
        for nm in ("l3_bq", "l3_bk", "l3_bd", "l3_gam", "l3_bet"):
            din(nm, [128, DC])
        din("l3_bv", [D])
        din("l3_wg", [2816, 1024]); din("l3_wu", [2816, 1024])
        din("l3_wdf", [1024, 2816])
        dscr("wgq_sp", [1024, 2816], BF16)
        dscr("wuq_sp", [1024, 2816], BF16)
        dscr("wdq_sp", [2816, 1024], BF16)
        dscr("xq_sp", [D, T], BF16)
        dscr("gate_sp", [2816, T])
        dscr("up_sp", [2816, T])
        dscr("h2_sp", [2816, T], BF16)
        dscr("hq_sp", [2816, T], BF16)

    dscr("xres", [D, T])
    dscr("bc_scr", [16, TG])
    dscr("wfm_dram", [8, T], BF16)
    dscr("xn_sp", [D, T], BF16)
    dscr("g_sp", [D, T])
    dscr("s_sp", [D, T])
    dscr("gacc", [D, T])
    dscr("q_sp", [D, T], BF16)
    dscr("k_sp", [D, T], BF16)
    dscr("o_att", [D, T], BF16)

    with ExitStack() as top:
        c.LD = Sem(top, nc, "LD")
        c.GD = Sem(top, nc, "GD")
        c.PE = Sem(top, nc, "PE")
        c.VE = Sem(top, nc, "VE")
        c.SE = Sem(top, nc, "SE")
        c.block = WrapBlock(top.enter_context(nc.Block()))

        c.es = top
        c.id_bf = sb(c, "id_bf", [128, 128], BF16)
        c.id_f = sb(c, "id_f", [128, 128], F32)
        c.cm = sb(c, "cmask", [128, 128], F32)
        c.ones_f = sb(c, "ones", [128, 1], F32)
        c.cc = sb(c, "cc", [128, 2], F32)
        c.ones_bf = sb(c, "ones_bf", [128, 1], BF16)
        c.zcol = sb(c, "zcol", [128, 1], F32)

        @c.block.sync
        def _(e):
            c.LD.dma(e.dma_start(out=c.id_bf[:], in_=d["ident_bf"]))
            c.LD.dma(e.dma_start(out=c.id_f[:], in_=d["ident_f"]))
            c.LD.dma(e.dma_start(out=c.cm[:], in_=d["cmask"]))
            c.LD.dma(e.dma_start(out=c.ones_f[:], in_=d["ones_col"]))
            c.LD.dma(e.dma_start(out=c.cc[:], in_=d["consts"]))
            c.LD.dma(e.dma_start(out=c.ones_bf[:], in_=d["ones_bfc"]))

        @c.block.scalar
        def _(e):
            e.memzero(c.zcol[:])
            c.SE.inc(e.last)
        barrier(c)

        phase_combine(c, d["x_fm"], False, None, True)
        if PHASE_LIMIT >= 2:
            phase_l0_router(c)
        if PHASE_LIMIT >= 3:
            phase_l0_moe(c)
            phase_silu_g(c, d["gacc"])
        lay0 = dict(wq="wq0", wk="wk0", wv="wv0", wo="wo0", scale=0.125)
        if PHASE_LIMIT >= 4:
            phase_mha(c, lay0)
        last = (n_layers == 1)
        phase_combine(c, d["x_fm"], True,
                      d["out_fm"] if last else d["xres"], not last)

        if n_layers >= 2:
            phase_l1_ffn(c)
            lay1 = dict(wq="wq1", wk="wk1", wv="wv1", wo="wo1",
                        bq="bq1", bk="bk1", bv="bv1", bo="bo1",
                        scale=1.0 / 32.0)
            phase_mha(c, lay1)
            last = (n_layers == 2)
            phase_combine(c, d["xres"], True,
                          d["out_fm"] if last else d["xres"], not last)

        if n_layers >= 3:
            phase_l2_mamba(c)
            phase_l2_moeattn(c)
            last = (n_layers == 3)
            phase_combine(c, d["xres"], True,
                          d["out_fm"] if last else d["xres"], not last)

        if n_layers >= 4:
            phase_l3_linattn(c)
            phase_l3_fp4(c)
            phase_combine(c, d["xres"], True, d["out_fm"], False)

        ldv = c.LD.v

        @c.block.sync
        def _(e):
            e.wait_ge(c.LD.h, ldv)

    return nc



_nc_cache = {}


def _prep_shared(inputs):
    f32 = np.float32
    s = {}
    s["ident_bf"] = np.eye(128, dtype=BF)
    s["ident_f"] = np.eye(128, dtype=f32)
    cm = np.zeros((128, 128), f32)
    iu = np.triu_indices(128, 1)
    cm[iu] = -1.0e30
    s["cmask"] = cm
    s["ones_col"] = np.ones((128, 1), f32)
    cc = np.zeros((128, 2), f32)
    cc[:, 0] = 1e-6
    s["consts"] = cc
    s["ones_bfc"] = np.ones((128, 1), BF)

    for nm, key in (("wq0", "L0_attn_w_q"), ("wk0", "L0_attn_w_k"),
                    ("wv0", "L0_attn_w_v")):
        w = inputs[key]
        s[nm] = np.ascontiguousarray(np.transpose(w, (2, 0, 1)).reshape(1024, 1024))
    s["wo0"] = np.ascontiguousarray(inputs["L0_attn_w_o"])
    s["w1"] = np.ascontiguousarray(inputs["L0_ffn_W1"])
    s["w2"] = np.ascontiguousarray(inputs["L0_ffn_W2"])
    b1 = inputs["L0_ffn_b1"]
    s["b1_fm"] = np.ascontiguousarray(
        b1.reshape(8, 32, 128).transpose(2, 0, 1).reshape(128, 256))
    s["b2"] = np.ascontiguousarray(inputs["L0_ffn_b2"])
    s["wr"] = np.ascontiguousarray(inputs["L0_ffn_Wr"])
    s["br_col"] = np.ascontiguousarray(inputs["L0_ffn_br"].reshape(8, 1))

    if NLAYERS >= 2:
        for nm, key in (("wq1", "L1_attn_Wq"), ("wk1", "L1_attn_Wk"),
                        ("wv1", "L1_attn_Wv"), ("wo1", "L1_attn_Wo")):
            s[nm] = np.ascontiguousarray(inputs[key])
        def col(v, oc):
            return np.ascontiguousarray(np.asarray(v).reshape(oc, 128).T)
        for nm, key in (("bq1", "L1_attn_bq"), ("bk1", "L1_attn_bk"),
                        ("bo1", "L1_attn_bo")):
            s[nm] = col(inputs[key], 8)
        s["bv1"] = np.ascontiguousarray(inputs["L1_attn_bv"])
        s["l1_wfc_t"] = np.ascontiguousarray(inputs["L1_ffn_W_fc"].T)
        s["l1_wpr_t"] = np.ascontiguousarray(inputs["L1_ffn_W_proj"].T)
        s["l1_lnw"] = col(inputs["L1_ffn_ln_w"], 32)
        s["l1_lnb"] = col(inputs["L1_ffn_ln_b"], 32)

    if NLAYERS >= 3:
        def col(v, oc):
            return np.ascontiguousarray(np.asarray(v).reshape(oc, 128).T)
        s["l2_ipw_t"] = np.ascontiguousarray(inputs["L2_attn_in_proj_w"].T)
        cw = np.asarray(inputs["L2_attn_conv_w"])[:, 0, :]
        s["l2_convw"] = np.ascontiguousarray(
            cw.reshape(16, 128, 4).transpose(1, 0, 2))
        s["l2_convb"] = col(inputs["L2_attn_conv_b"], 16)
        s["l2_xpw_t"] = np.ascontiguousarray(inputs["L2_attn_x_proj_w"].T)
        s["l2_dthw_t"] = np.ascontiguousarray(inputs["L2_attn_dt_head_w"].T)
        s["l2_dthb"] = col(inputs["L2_attn_dt_head_b"], 16)
        s["l2_opw_t"] = np.ascontiguousarray(inputs["L2_attn_out_proj_w"].T)
        s["l2_D"] = col(inputs["L2_attn_D"], 16)
        s["l2_sim"] = np.ascontiguousarray(inputs["L2_ffn_sim_matrix"])
        s["l2_gates"] = np.ascontiguousarray(inputs["L2_ffn_gates"])
        s["l2_qp"] = np.ascontiguousarray(inputs["L2_ffn_q_proj"])
        s["l2_kp"] = np.ascontiguousarray(inputs["L2_ffn_k_proj"])
        s["l2_vp"] = np.ascontiguousarray(inputs["L2_ffn_v_proj"])
        s["l2_op"] = np.ascontiguousarray(inputs["L2_ffn_o_proj"])

    if NLAYERS >= 4:
        def col(v, oc):
            return np.ascontiguousarray(np.asarray(v).reshape(oc, 128).T)
        for nm, key in (("l3_wq_t", "L3_attn_Wq"), ("l3_wk_t", "L3_attn_Wk"),
                        ("l3_wv_t", "L3_attn_Wv"), ("l3_wd_t", "L3_attn_Wd")):
            s[nm] = np.ascontiguousarray(inputs[key].T)
        for nm, key in (("l3_bq", "L3_attn_bq"), ("l3_bk", "L3_attn_bk"),
                        ("l3_bd", "L3_attn_bd"), ("l3_gam", "L3_attn_gamma"),
                        ("l3_bet", "L3_attn_beta")):
            s[nm] = col(inputs[key], 8)
        s["l3_bv"] = np.ascontiguousarray(inputs["L3_attn_bv"])
        s["l3_wg"] = np.ascontiguousarray(inputs["L3_ffn_w_gate"])
        s["l3_wu"] = np.ascontiguousarray(inputs["L3_ffn_w_up"])
        s["l3_wdf"] = np.ascontiguousarray(inputs["L3_ffn_w_down"])
    return {k: np.asarray(v) for k, v in s.items()}


def kernel(**inputs):
    n_layers = NLAYERS
    key = (n_layers, PHASE_LIMIT, CMB_STEPS, NTG_LIM, ATT_STEPS, DEBUG_TAPS)
    if key not in _nc_cache:
        _nc_cache[key] = build_nc(n_layers)
    nc = _nc_cache[key]

    shared = _prep_shared(inputs)
    x = np.asarray(inputs["L0_attn_x_q"], dtype=np.float32)

    in_maps = []
    for core in range(8):
        m = dict(shared)
        xc = x[2 * core:2 * core + 2]
        m["x_fm"] = np.ascontiguousarray(
            np.transpose(xc, (2, 0, 1)).reshape(D, T))
        in_maps.append(m)

    res = run_bass_kernel_spmd(nc, in_maps, core_ids=list(range(8)))
    if DEBUG_TAPS:
        kernel.dbg = res.results
    outs = []
    for core in range(8):
        o = res.results[core]["out_fm"]
        o = o.reshape(D, NS, LSEQ).transpose(1, 2, 0)
        outs.append(o)
    return np.ascontiguousarray(np.concatenate(outs, axis=0), dtype=np.float32)
